# version 01
# baseline (speedup 1.0000x reference)
import math
import sys
from contextlib import ExitStack

import numpy as np
import ml_dtypes

if "/opt/trn_rl_repo" not in sys.path:
    sys.path.insert(0, "/opt/trn_rl_repo")

import concourse.bass as bass
import concourse.mybir as mybir
from concourse.bass_utils import run_bass_kernel_spmd

F32 = mybir.dt.float32
BF16 = mybir.dt.bfloat16
I32 = mybir.dt.int32
AF = mybir.ActivationFunctionType
ALU = mybir.AluOpType
AX = mybir.AxisListType
BF = ml_dtypes.bfloat16

NCORES = 8
D = 1024
NTOK = 4096
TOWN = 2048
TG = 512
EPS = 1e-5
DT_HEB = 0.01
ETA = 0.1



class K:
    def __init__(self, name):
        self.name = name
        self.nc = bass.Bass("TRN2", target_bir_lowering=False)
        self.ctx = ExitStack()
        self.blk = None
        self.cnt = {"pe": 0, "dve": 0, "act": 0, "sd": 0, "gd": 0, "st": 0}
        self.sem = {}
        self.sb_names = set()
        self._psum = []
        self._psum_i = 0

    def din(self, name, shape, dtype=F32):
        return self.nc.dram_tensor(name, list(shape), dtype, kind="ExternalInput").ap()

    def dout(self, name, shape, dtype=F32):
        return self.nc.dram_tensor(name, list(shape), dtype, kind="ExternalOutput").ap()

    def dint(self, name, shape, dtype=F32):
        return self.nc.dram_tensor(name, list(shape), dtype, kind="Internal").ap()

    def sb(self, name, shape, dtype=F32):
        assert name not in self.sb_names, name
        self.sb_names.add(name)
        return self.ctx.enter_context(self.nc.sbuf_tensor(name, list(shape), dtype))

    def start(self):
        for s in ["pe", "dve", "act", "sd", "gd", "st"]:
            self.sem[s] = self.ctx.enter_context(self.nc.semaphore("sem_" + s))
        for i in range(8):
            self._psum.append(self.ctx.enter_context(
                self.nc.psum_tensor(f"ps{i}", [128, 512], F32)))
        self.blk = self.ctx.enter_context(self.nc.Block())

    def psum(self):
        t = self._psum[self._psum_i % 7]
        self._psum_i += 1
        return t

    def psum_pin(self):
        return self._psum[7]

    def finish(self):
        st = self.cnt["st"]
        sem = self.sem["st"]
        self.blk.sync(lambda s: s.wait_ge(sem, st))
        self.ctx.close()
        return self.nc

    def _waits(self, eng, which):
        for w in which:
            if self.cnt[w] > 0:
                eng.wait_ge(self.sem[w], self.cnt[w])

    def _emit(self, stream, cname, inc, f, deps):
        sem = self.sem[cname]
        def g(e):
            self._waits(e, deps)
            last = f(e)
            last.then_inc(sem, inc)
        stream(g)
        self.cnt[cname] += inc

    def pe(self, f, deps=("dve", "act", "sd", "gd", "st")):
        self._emit(self.blk.tensor, "pe", 1, f, deps)

    def dve(self, f, deps=("pe", "act", "sd", "gd", "st")):
        self._emit(self.blk.vector, "dve", 1, f, deps)

    def act(self, f, deps=("pe", "dve", "sd", "gd", "st")):
        self._emit(self.blk.scalar, "act", 1, f, deps)

    def sdma(self, f, deps=("pe", "dve", "act", "st")):
        self._emit(self.blk.sync, "sd", 16, f, deps)

    def gdma(self, f, deps=("pe", "dve", "act", "st")):
        self._emit(self.blk.gpsimd, "gd", 16, f, deps)

    def gp(self, f, deps=("pe", "dve", "act", "sd", "st")):
        self._emit(self.blk.gpsimd, "gd", 16, f, deps)

    def store(self, f, deps=("pe", "dve", "act", "gd")):
        self._emit(self.blk.sync, "st", 16, f, deps)

    def ts(self, out, in0, s1, op, s2=None, op1=None):
        if op1 is None:
            if op == ALU.add:
                self.dve(lambda v: v.tensor_scalar_add(out=out, in0=in0, scalar1=s1))
            elif op == ALU.mult:
                self.dve(lambda v: v.tensor_scalar_mul(out=out, in0=in0, scalar1=s1))
            elif op == ALU.max:
                self.dve(lambda v: v.tensor_scalar_max(out=out, in0=in0, scalar1=s1))
            elif op == ALU.subtract:
                self.dve(lambda v: v.tensor_scalar_sub(out=out, in0=in0, scalar1=s1))
            elif op == ALU.min:
                self.dve(lambda v: v.tensor_scalar_min(out=out, in0=in0, scalar1=s1))
            elif op == ALU.mod:
                self.dve(lambda v: v.tensor_scalar(out=out, in0=in0, scalar1=s1,
                                                   scalar2=None, op0=ALU.mod))
            else:
                raise ValueError(op)
        else:
            self.dve(lambda v: v.tensor_scalar(out=out, in0=in0, scalar1=s1,
                                               scalar2=s2, op0=op, op1=op1))

    def tt(self, out, in0, in1, op):
        self.dve(lambda v: v.tensor_tensor(out=out, in0=in0, in1=in1, op=op))

    def cp(self, out, in_):
        self.dve(lambda v: v.tensor_copy(out=out, in_=in_))

    def recip(self, out, in_):
        self.dve(lambda v: v.reciprocal(out=out, in_=in_))

    def actf(self, out, in_, func, bias=None, scale=1.0):
        if bias is None:
            self.act(lambda a: a.activation(out=out, in_=in_, func=func, scale=scale))
        else:
            self.act(lambda a: a.activation(out=out, in_=in_, func=func, bias=bias,
                                            scale=scale))



def tile_lhsT(w):
    w = np.asarray(w, dtype=np.float32)
    Din, Dout = w.shape
    nKc = (Din + 127) // 128
    nMb = (Dout + 127) // 128
    wp = np.zeros((nKc * 128, nMb * 128), dtype=np.float32)
    wp[:Din, :Dout] = w
    out = np.zeros((nMb, nKc, 128, 128), dtype=np.float32)
    for mb in range(nMb):
        for kc in range(nKc):
            out[mb, kc] = wp[kc * 128:(kc + 1) * 128, mb * 128:(mb + 1) * 128]
    return np.ascontiguousarray(out)


def col_bias(b, nMb):
    b = np.asarray(b, dtype=np.float32)
    out = np.zeros((128, nMb), dtype=np.float32)
    for mb in range(nMb):
        seg = b[mb * 128:(mb + 1) * 128]
        out[:len(seg), mb] = seg
    return out


def gb_rows(g, b):
    return np.ascontiguousarray(np.concatenate(
        [np.asarray(g, np.float32).ravel(), np.asarray(b, np.float32).ravel()])[None, :])


def fmh(x):
    return np.ascontiguousarray(np.asarray(x, dtype=np.float32).T)



class Consts:
    pass


def emit_consts(k):
    c = Consts()
    c.col = k.sb("c_onescol", [128, 1], F32)
    c.row = k.sb("c_onesrow", [1, 128], F32)
    c.rowb = k.sb("c_onesrowb", [1, 128], BF16)
    c.eps = k.sb("c_eps", [1, 1], F32)
    c.one = k.sb("c_one1", [1, 1], F32)
    c.onerow = k.sb("c_onerow", [1, TG], F32)
    k.gp(lambda s: s.memset(c.col[:], 1.0))
    k.gp(lambda s: s.memset(c.row[:], 1.0))
    k.gp(lambda s: s.memset(c.rowb[:], 1.0))
    k.gp(lambda s: s.memset(c.eps[:], EPS))
    k.gp(lambda s: s.memset(c.one[:], 1.0))
    k.gp(lambda s: s.memset(c.onerow[:], 1.0))
    return c


class LN:

    def __init__(self, k, name, c, nKc=8):
        self.k, self.nKc, self.c = k, nKc, c
        self.rows = k.sb(name + "_rows", [1, 4 * TG], F32)

    def m(self, tw):
        return self.rows[:, 0:tw]

    def r(self, tw):
        return self.rows[:, TG:TG + tw]

    def o(self, tw):
        return self.rows[:, 2 * TG:2 * TG + tw]

    def stats_tg(self, xcols, tw, sq):
        k, nKc = self.k, self.nKc
        ps_s = k.psum()
        ps_q = k.psum()
        def fsum(t):
            last = None
            for kc in range(nKc):
                last = t.matmul(out=ps_s[0:1, 0:tw], lhsT=self.c.col[:, 0:1],
                                rhs=xcols(kc), start=(kc == 0), stop=(kc == nKc - 1))
            return last
        k.pe(fsum)
        for kc in range(nKc):
            k.actf(sq[:, 0:tw], xcols(kc), AF.Square)
            k.pe(lambda t, kc=kc: t.matmul(
                out=ps_q[0:1, 0:tw], lhsT=self.c.col[:, 0:1], rhs=sq[:, 0:tw],
                start=(kc == 0), stop=(kc == nKc - 1)))
        Dtot = nKc * 128
        m = self.m(tw); r = self.r(tw); o = self.o(tw)
        scr = self.rows[:, 3 * TG:3 * TG + tw]
        k.ts(m, ps_s[0:1, 0:tw], 1.0 / Dtot, ALU.mult)
        k.ts(scr, ps_q[0:1, 0:tw], 1.0 / Dtot, ALU.mult)
        k.tt(r, m, m, ALU.mult)
        k.tt(scr, scr, r, ALU.subtract)
        k.actf(scr, scr, AF.Sqrt, bias=self.c.eps[:])
        k.recip(r, scr)
        k.tt(o, m, r, ALU.mult)
        k.ts(o, o, -1.0, ALU.mult)

    def apply_tg(self, gbrow, tw, consume, nKc=None, D0=None):
        k = self.k
        nKc = nKc or self.nKc
        D0 = D0 or (nKc * 128)
        r = self.r(tw)
        o = self.o(tw)
        for kc in range(nKc):
            ps_sc = k.psum()
            ps_of = k.psum()
            k.pe(lambda t, kc=kc: t.matmul(
                out=ps_sc[0:128, 0:tw], lhsT=gbrow[0:1, kc * 128:(kc + 1) * 128],
                rhs=r, start=True, stop=True))
            def foff(t, kc=kc):
                t.matmul(out=ps_of[0:128, 0:tw],
                         lhsT=gbrow[0:1, kc * 128:(kc + 1) * 128], rhs=o,
                         start=True, stop=False)
                return t.matmul(out=ps_of[0:128, 0:tw],
                                lhsT=gbrow[0:1, D0 + kc * 128:D0 + (kc + 1) * 128],
                                rhs=self.c.onerow[:, 0:tw], start=False, stop=True)
            k.pe(foff)
            consume(kc, ps_sc[0:128, 0:tw], ps_of[0:128, 0:tw])

    def plain_apply_tg(self, tw, xcols, outcols):
        k = self.k
        ps_r = k.psum(); ps_o = k.psum()
        k.pe(lambda t: t.matmul(out=ps_r[0:128, 0:tw], lhsT=self.c.row[:, 0:128],
                                rhs=self.r(tw), start=True, stop=True))
        k.pe(lambda t: t.matmul(out=ps_o[0:128, 0:tw], lhsT=self.c.row[:, 0:128],
                                rhs=self.o(tw), start=True, stop=True))
        for kc in range(self.nKc):
            k.tt(outcols(kc), xcols(kc), ps_r[0:128, 0:tw], ALU.mult)
            k.tt(outcols(kc), outcols(kc), ps_o[0:128, 0:tw], ALU.add)


def ln_consume_to(k, xcols, scratch, outcols, tw=TG):
    def consume(kc, sc, of):
        k.tt(scratch[:, 0:tw], xcols(kc), sc, ALU.mult)
        k.tt(outcols(kc), scratch[:, 0:tw], of, ALU.add)
    return consume



def build_prep():
    k = K("prep")
    x = k.din("x", [D, NTOK])
    ln1gb = k.din("ln1gb", [1, 2 * D])
    wpot = k.din("wpot", [1, 8, 128, 128])
    bpot = k.din("bpot", [1, 2])
    xn_out = k.dout("xn", [D, NTOK])
    d2_out = k.dout("d2", [NTOK, 2])
    d2r_out = k.dout("d2r", [NTOK, 2])
    gam_out = k.dout("gamma", [1, NTOK])
    k.start()
    C = emit_consts(k)

    xf = k.sb("xf", [128, 8 * TG], F32)
    xnf = k.sb("xnf", [128, 8 * TG], F32)
    xb = k.sb("xb", [128, 8 * TG], BF16)
    sq = k.sb("sq", [128, TG], F32)
    scr = k.sb("scr", [128, TG], F32)
    g1 = k.sb("g1", [1, 2 * D], F32)
    wp = k.sb("wp", [128, 8 * 2], BF16)
    bp = k.sb("bp", [1, 2], F32)
    potg = k.sb("potg", [1, 2 * TG], F32)
    grow = k.sb("grow", [1, 4 * TG], F32)
    d2row = k.sb("d2row", [1, 2 * TG], F32)
    d2rev = k.sb("d2rev", [1, 2 * TG], F32)

    k.sdma(lambda s: s.dma_start(out=g1[:], in_=ln1gb[:]))
    wpv = wp.ap().rearrange("p (kc m) -> p kc m", kc=8)
    k.gdma(lambda s: s.dma_start(out=wpv,
                                 in_=wpot.rearrange("o kc p m -> p (o kc) m")[:, :, 0:2]))
    k.sdma(lambda s: s.dma_start(out=bp[:], in_=bpot[:]))

    x_fm = x
    topln = LN(k, "topln", C)
    ln1 = LN(k, "ln1", C)
    xcols = lambda kc: xf[:, kc * TG:(kc + 1) * TG]
    xncols = lambda kc: xnf[:, kc * TG:(kc + 1) * TG]
    xbcols = lambda kc: xb[:, kc * TG:(kc + 1) * TG]

    for t0 in range(0, NTOK, TG):
        for kc in range(8):
            k.sdma(lambda s, kc=kc, t0=t0: s.dma_start(
                out=xf[:, kc * TG:(kc + 1) * TG],
                in_=x_fm[kc * 128:(kc + 1) * 128, t0:t0 + TG]))
        topln.stats_tg(xcols, TG, sq)
        topln.plain_apply_tg(TG, xcols, xncols)
        for kc in range(8):
            k.store(lambda s, kc=kc, t0=t0: s.dma_start(
                out=xn_out[kc * 128:(kc + 1) * 128, t0:t0 + TG], in_=xncols(kc)))
        ln1.stats_tg(xncols, TG, sq)
        ln1.apply_tg(g1, TG, ln_consume_to(k, xncols, scr, xbcols))
        ps_v = k.psum()
        ps_g = k.psum()
        def fp(t, t0=t0, col=0, ps=None):
            last = None
            for kc in range(8):
                last = t.matmul(out=ps[0:1, 0:TG],
                                lhsT=wp[:, kc * 2 + col:kc * 2 + col + 1],
                                rhs=xb[:, kc * TG:(kc + 1) * TG],
                                start=(kc == 0), stop=(kc == 7))
            return last
        k.pe(lambda t, t0=t0: fp(t, t0, 0, ps_v))
        k.pe(lambda t, t0=t0: fp(t, t0, 1, ps_g))
        k.ts(potg[0:1, 0:TG], ps_v[0:1, 0:TG], bp[:, 0:1], ALU.add)
        k.ts(potg[0:1, TG:2 * TG], ps_g[0:1, 0:TG], bp[:, 1:2], ALU.add)

        p1 = potg[0:1, TG:2 * TG]
        t_abs = grow[:, 0:TG]
        t_e = grow[:, TG:2 * TG]
        gam = grow[:, 2 * TG:3 * TG]
        k.actf(t_abs, p1, AF.Abs)
        k.actf(t_e, t_abs, AF.Exp, scale=-1.0)
        k.actf(t_e, t_e, AF.Ln, bias=C.one[:])
        k.ts(gam, p1, 0.0, ALU.max)
        k.tt(gam, gam, t_e, ALU.add)
        k.ts(gam, gam, 0.1, ALU.add)
        k.store(lambda s, t0=t0, gam=gam: s.dma_start(
            out=gam_out[:, t0:t0 + TG], in_=gam))

        d2v = d2row.ap().rearrange("o (t r) -> o t r", r=2)
        k.ts(d2v[:, :, 0:1], potg[0:1, 0:TG].rearrange("o (t u) -> o t u", u=1), 2.0, ALU.add)
        k.ts(d2v[:, :, 1:2], gam.rearrange("o (t u) -> o t u", u=1), -1.0, ALU.mult)
        k.store(lambda s, t0=t0: s.dma_start(
            out=d2_out[t0:t0 + TG, :].rearrange("t r -> (t r)").rearrange("(u x) -> u x", u=1), in_=d2row[:]))
        d2rv = d2rev.ap().rearrange("o (t r) -> o t r", r=2)
        k.cp(d2rv[:, :, :], d2v[:, ::-1, :])
        k.store(lambda s, t0=t0: s.dma_start(
            out=d2r_out[NTOK - t0 - TG:NTOK - t0, :].rearrange("t r -> (t r)").rearrange("(u x) -> u x", u=1),
            in_=d2rev[:]))
    k.finish()
    return k



def build_cf():
    k = K("cf")
    d2 = k.din("d2", [NTOK, 2])
    d2r = k.din("d2r", [NTOK, 2])
    sgn4in = k.din("sgn4", [128, 4])
    sgn2in = k.din("sgn2", [128, 2])
    rev64in = k.din("rev64", [128, 64])
    Gp = k.dout("Gp", [2, NTOK])
    dbg_tot = k.dout("dbg_tot", [128, 8])
    dbg_mh = k.dout("dbg_mh", [16, 80])
    dbg_vh = k.dout("dbg_vh", [2, 36])
    dbg_av = k.dout("dbg_av", [16, 16])
    dbg_ap0 = k.dout("dbg_ap0", [128, 2])
    dbg_ah = k.dout("dbg_ah", [128, 128])
    scr1 = k.dint("scr1", [128, 8])
    scr2 = k.dint("scr2", [16, 8])
    scr3 = k.dint("scr3", [2, 32])
    scr4 = k.dint("scr4", [16, 16])
    k.start()
    CH = 64
    Dt = k.sb("Dt", [128, CH * 2], F32)
    sgn4 = k.sb("sb_sgn4", [128, 4], F32)
    sgn2 = k.sb("sb_sgn2", [128, 2], F32)
    rev64 = k.sb("sb_rev64", [128, 64], F32)
    k.sdma(lambda s: s.dma_start(out=sgn4[:], in_=sgn4in[:]))
    k.sdma(lambda s: s.dma_start(out=sgn2[:], in_=sgn2in[:]))
    k.sdma(lambda s: s.dma_start(out=rev64[:], in_=rev64in[:]))
    Dtv = Dt.ap().rearrange("p (s r) -> p s r", r=2)
    k.sdma(lambda s: s.dma_start(out=Dtv[0:64],
                                 in_=d2.rearrange("(c s) r -> c s r", c=64)))
    k.sdma(lambda s: s.dma_start(out=Dtv[64:128],
                                 in_=d2r.rearrange("(c s) r -> c s r", c=64)))

    st = k.sb("cf_state", [128, 12], F32)
    tmp = k.sb("cf_tmp", [128, 16], F32)
    nrm = k.sb("cf_nrm", [128, 2], F32)
    bufs = [st[:, 0:4], st[:, 4:8], st[:, 8:12]]
    k.gp(lambda s: s.memset(st[:, 0:12], 0.0))
    k.gp(lambda s: s.memset(st[:, 0:1], 1.0))
    k.gp(lambda s: s.memset(st[:, 6:7], 1.0))
    cur, prv, nxt = 0, 1, 2
    for s_i in range(CH):
        dr = Dt[:, 2 * s_i:2 * s_i + 1]
        di = Dt[:, 2 * s_i + 1:2 * s_i + 2]
        cur_ap, prv_ap, nxt_ap = bufs[cur], bufs[prv], bufs[nxt]
        t1, t2, t3, t4 = tmp[:, 0:4], tmp[:, 4:8], tmp[:, 8:12], tmp[:, 12:16]
        k.ts(t1, cur_ap, dr, ALU.mult)
        cur_sw = cur_ap.rearrange("p (a b) -> p a b", b=2)[:, :, ::-1]
        k.ts(t2.rearrange("p (a b) -> p a b", b=2), cur_sw, di, ALU.mult)
        k.tt(t3, t2, sgn4[:], ALU.mult)
        k.tt(t4, t1, t3, ALU.add)
        k.tt(nxt_ap, t4, prv_ap, ALU.subtract)
        if s_i % 4 == 3:
            k.tt(tmp[:, 0:4], nxt_ap, nxt_ap, ALU.mult)
            k.tt(nrm[:, 0:2], tmp[:, 0:2], tmp[:, 2:4], ALU.add)
            k.tt(nrm[:, 0:1], nrm[:, 0:1], nrm[:, 1:2], ALU.add)
            k.ts(nrm[:, 0:1], nrm[:, 0:1], 1e-30, ALU.max)
            k.actf(nrm[:, 0:1], nrm[:, 0:1], AF.Sqrt)
            k.recip(nrm[:, 0:1], nrm[:, 0:1])
            k.ts(nxt_ap, nxt_ap, nrm[:, 0:1], ALU.mult)
            k.ts(cur_ap, cur_ap, nrm[:, 0:1], ALU.mult)
        cur, prv, nxt = nxt, cur, prv
    tot = k.sb("cf_tot", [128, 8], F32)
    k.cp(tot[:, 0:4], bufs[cur])
    k.cp(tot[:, 4:8], bufs[prv])
    k.store(lambda s: s.dma_start(out=scr1[:], in_=tot[:]))
    k.store(lambda s: s.dma_start(out=dbg_tot[:], in_=tot[:]))

    Tt = k.sb("cf_Tt", [16, 64], F32)
    k.sdma(lambda s: s.dma_start(out=Tt.ap().rearrange("p (j v) -> p j v", v=8),
                                 in_=scr1.rearrange("(dg j) v -> dg j v", j=8)),
           deps=("pe", "dve", "act", "st"))
    Mh = k.sb("cf_Mh", [16, 80], F32)
    k.gp(lambda s: s.memset(Mh[:, 0:80], 0.0))
    k.gp(lambda s: s.memset(Mh[:, 0:1], 1.0))
    k.gp(lambda s: s.memset(Mh[:, 6:7], 1.0))
    mtmp = k.sb("cf_mtmp", [16, 24], F32)
    mnrm = k.sb("cf_mnrm", [16, 2], F32)

    def cmulM(out_ap, row_ap, tre, tim, lanes):
        w = row_ap.shape[-1]
        a = mtmp[0:lanes, 16:16 + w]
        b2 = mtmp[0:lanes, 20:20 + w]
        k.ts(a, row_ap, tre, ALU.mult)
        row_sw = row_ap.rearrange("p (x y) -> p x y", y=2)[:, :, ::-1]
        k.ts(b2.rearrange("p (x y) -> p x y", y=2), row_sw, tim, ALU.mult)
        k.tt(b2, b2, sgn4[0:lanes, 0:w], ALU.mult)
        k.tt(out_ap, a, b2, ALU.add)

    for j in range(8):
        mprev = Mh[:, j * 8:(j + 1) * 8]
        mnew = Mh[:, (j + 1) * 8:(j + 2) * 8]
        for oi in range(2):
            t_re0 = Tt[:, 8 * j + 4 * oi + 0:8 * j + 4 * oi + 1]
            t_im0 = Tt[:, 8 * j + 4 * oi + 1:8 * j + 4 * oi + 2]
            t_re1 = Tt[:, 8 * j + 4 * oi + 2:8 * j + 4 * oi + 3]
            t_im1 = Tt[:, 8 * j + 4 * oi + 3:8 * j + 4 * oi + 4]
            p0 = mtmp[0:16, 0:4]
            p1 = mtmp[0:16, 4:8]
            cmulM(p0, mprev[:, 0:4], t_re0, t_im0, 16)
            cmulM(p1, mprev[:, 4:8], t_re1, t_im1, 16)
            k.tt(mnew[:, 4 * oi:4 * oi + 4], p0, p1, ALU.add)
        k.tt(mtmp[0:16, 8:12], mnew[:, 0:4], mnew[:, 0:4], ALU.mult)
        k.tt(mnrm[:, 0:2], mtmp[0:16, 8:10], mtmp[0:16, 10:12], ALU.add)
        k.tt(mnrm[:, 0:1], mnrm[:, 0:1], mnrm[:, 1:2], ALU.add)
        k.ts(mnrm[:, 0:1], mnrm[:, 0:1], 1e-30, ALU.max)
        k.actf(mnrm[:, 0:1], mnrm[:, 0:1], AF.Sqrt)
        k.recip(mnrm[:, 0:1], mnrm[:, 0:1])
        k.ts(mnew, mnew, mnrm[:, 0:1], ALU.mult)
    k.store(lambda s: s.dma_start(out=scr2[:], in_=Mh[:, 64:72]))
    k.store(lambda s: s.dma_start(out=dbg_mh[:], in_=Mh[:]))

    Gt = k.sb("cf_Gt", [2, 64], F32)
    k.sdma(lambda s: s.dma_start(out=Gt.ap().rearrange("p (g v) -> p g v", v=8),
                                 in_=scr2.rearrange("(d g) v -> d g v", g=8)),
           deps=("pe", "dve", "act", "st"))
    vh = k.sb("cf_vh", [2, 36], F32)
    k.gp(lambda s: s.memset(vh[:, 0:36], 0.0))
    k.gp(lambda s: s.memset(vh[:, 0:1], 1.0))
    vtmp = k.sb("cf_vtmp", [16, 24], F32)
    vnrm = k.sb("cf_vnrm", [2, 2], F32)

    def cmulV(out_ap, vec_ap, tre, tim, lanes):
        a = vtmp[0:lanes, 8:10]
        b2 = vtmp[0:lanes, 10:12]
        k.ts(a, vec_ap, tre, ALU.mult)
        k.ts(b2, vec_ap[:, ::-1], tim, ALU.mult)
        k.tt(b2, b2, sgn4[0:lanes, 0:2], ALU.mult)
        k.tt(out_ap, a, b2, ALU.add)

    for gi in range(8):
        vprev = vh[:, gi * 4:gi * 4 + 4]
        vnew = vh[:, (gi + 1) * 4:(gi + 1) * 4 + 4]
        for oi in range(2):
            t_re0 = Gt[:, 8 * gi + 4 * oi + 0:8 * gi + 4 * oi + 1]
            t_im0 = Gt[:, 8 * gi + 4 * oi + 1:8 * gi + 4 * oi + 2]
            t_re1 = Gt[:, 8 * gi + 4 * oi + 2:8 * gi + 4 * oi + 3]
            t_im1 = Gt[:, 8 * gi + 4 * oi + 3:8 * gi + 4 * oi + 4]
            p0 = vtmp[0:2, 0:2]
            p1 = vtmp[0:2, 2:4]
            cmulV(p0, vprev[:, 0:2], t_re0, t_im0, 2)
            cmulV(p1, vprev[:, 2:4], t_re1, t_im1, 2)
            k.tt(vnew[:, 2 * oi:2 * oi + 2], p0, p1, ALU.add)
        k.tt(vtmp[0:2, 12:16], vnew, vnew, ALU.mult)
        k.tt(vnrm[:, 0:2], vtmp[0:2, 12:14], vtmp[0:2, 14:16], ALU.add)
        k.tt(vnrm[:, 0:1], vnrm[:, 0:1], vnrm[:, 1:2], ALU.add)
        k.ts(vnrm[:, 0:1], vnrm[:, 0:1], 1e-30, ALU.max)
        k.actf(vnrm[:, 0:1], vnrm[:, 0:1], AF.Sqrt)
        k.recip(vnrm[:, 0:1], vnrm[:, 0:1])
        k.ts(vnew, vnew, vnrm[:, 0:1], ALU.mult)
    k.store(lambda s: s.dma_start(out=scr3[:], in_=vh[:, 0:32]))
    k.store(lambda s: s.dma_start(out=dbg_vh[:], in_=vh[:]))

    gst = k.sb("cf_gst", [16, 4], F32)
    k.sdma(lambda s: s.dma_start(out=gst[:],
                                 in_=scr3.rearrange("d (g v) -> (d g) v", v=4)),
           deps=("pe", "dve", "act", "st"))
    av = k.sb("cf_av", [16, 16], F32)
    for j in range(8):
        Mj = Mh[:, j * 8:(j + 1) * 8]
        vres = vtmp[0:16, 0:4]
        for oi in range(2):
            t_re0 = Mj[:, 4 * oi + 0:4 * oi + 1]
            t_im0 = Mj[:, 4 * oi + 1:4 * oi + 2]
            t_re1 = Mj[:, 4 * oi + 2:4 * oi + 3]
            t_im1 = Mj[:, 4 * oi + 3:4 * oi + 4]
            p0 = vtmp[0:16, 4:6]
            p1 = vtmp[0:16, 6:8]
            cmulV(p0, gst[:, 0:2], t_re0, t_im0, 16)
            cmulV(p1, gst[:, 2:4], t_re1, t_im1, 16)
            k.tt(vres[:, 2 * oi:2 * oi + 2], p0, p1, ALU.add)
        dsq = vtmp[0:16, 12:14]
        dn = vtmp[0:16, 14:15]
        k.tt(dsq, vres[:, 2:4], vres[:, 2:4], ALU.mult)
        k.tt(dn, dsq[:, 0:1], dsq[:, 1:2], ALU.add)
        k.ts(dn, dn, 1e-30, ALU.max)
        k.recip(dn, dn)
        prod = vtmp[0:16, 16:18]
        k.tt(prod, vres[:, 0:2], vres[:, 2:4], ALU.mult)
        k.tt(av[:, 2 * j:2 * j + 1], prod[:, 0:1], prod[:, 1:2], ALU.add)
        k.tt(prod, vres[:, 0:2][:, ::-1], vres[:, 2:4], ALU.mult)
        k.tt(av[:, 2 * j + 1:2 * j + 2], prod[:, 0:1], prod[:, 1:2], ALU.subtract)
        k.ts(av[:, 2 * j:2 * j + 2], av[:, 2 * j:2 * j + 2], dn, ALU.mult)
    k.store(lambda s: s.dma_start(out=scr4[:], in_=av[:]))
    k.store(lambda s: s.dma_start(out=dbg_av[:], in_=av[:]))

    ap0 = k.sb("cf_ap0", [128, 2], F32)
    k.sdma(lambda s: s.dma_start(out=ap0[:],
                                 in_=scr4.rearrange("dg (j c) -> (dg j) c", c=2)),
           deps=("pe", "dve", "act", "st"))
    k.gp(lambda s: s.memset(ap0[0:1, 0:1], 1e30))
    k.gp(lambda s: s.memset(ap0[0:1, 1:2], 0.0))
    k.gp(lambda s: s.memset(ap0[64:65, 0:1], 1e30))
    k.gp(lambda s: s.memset(ap0[64:65, 1:2], 0.0))
    k.store(lambda s: s.dma_start(out=dbg_ap0[:], in_=ap0[:]), deps=("pe", "dve", "act", "gd"))
    Ah = k.sb("cf_Ah", [128, CH * 2], F32)
    vt2 = k.sb("cf_vt2", [128, 8], F32)
    aprev = ap0[:, 0:2]
    for s_i in range(CH):
        sq2, ns, tq = vt2[:, 0:2], vt2[:, 2:3], vt2[:, 4:6]
        k.tt(sq2, aprev, aprev, ALU.mult)
        k.tt(ns, vt2[:, 0:1], vt2[:, 1:2], ALU.add)
        k.ts(ns, ns, 1e-30, ALU.max)
        k.recip(ns, ns)
        k.ts(tq, aprev, ns, ALU.mult)
        k.tt(tq, tq, sgn2[:], ALU.mult)
        anew = Ah[:, 2 * s_i:2 * s_i + 2]
        k.tt(anew, Dt[:, 2 * s_i:2 * s_i + 2], tq, ALU.subtract)
        aprev = anew

    k.store(lambda s: s.dma_start(out=dbg_ah[:], in_=Ah[:]))
    ps_b = k.psum()
    k.pe(lambda t: t.matmul(out=ps_b[0:64, 0:128], lhsT=rev64[64:128, :],
                            rhs=Ah[64:128, :], start=True, stop=True))
    Bt = k.sb("cf_Bt", [64, CH * 2], F32)
    k.cp(Bt.ap().rearrange("p (s r) -> p s r", r=2),
         ps_b[0:64, 0:128].rearrange("p (s r) -> p s r", r=2)[:, ::-1, :])
    gs = k.sb("cf_gs", [64, CH * 2], F32)
    sq3 = k.sb("cf_sq3", [64, CH * 2], F32)
    rec3 = k.sb("cf_rec3", [64, CH], F32)
    recx = k.sb("cf_recx", [64, CH * 2], F32)
    k.tt(gs[:], Ah[0:64, :], Bt[:], ALU.add)
    k.tt(gs[:], gs[:], Dt[0:64, :], ALU.subtract)
    k.tt(sq3[:], gs[:], gs[:], ALU.mult)
    sq3v = sq3.ap().rearrange("p (s r) -> p s r", r=2)
    k.tt(rec3.ap().rearrange("p (s u) -> p s u", u=1), sq3v[:, :, 0:1], sq3v[:, :, 1:2],
         ALU.add)
    k.ts(rec3[:], rec3[:], 1e-30, ALU.max)
    k.recip(rec3[:], rec3[:])
    rxv = recx.ap().rearrange("p (s r) -> p s r", r=2)
    k.cp(rxv[:, :, 0:1], rec3.ap().rearrange("p (s u) -> p s u", u=1))
    k.cp(rxv[:, :, 1:2], rec3.ap().rearrange("p (s u) -> p s u", u=1))
    k.tt(sq3[:], gs[:], recx[:], ALU.mult)
    k.ts(sq3v[:, :, 1:2], sq3v[:, :, 1:2], -1.0, ALU.mult)
    gplane = k.sb("cf_gplane", [64, 2 * CH], F32)
    k.cp(gplane[:, 0:CH].rearrange("p (s u) -> p s u", u=1), sq3v[:, :, 0:1])
    k.cp(gplane[:, CH:2 * CH].rearrange("p (s u) -> p s u", u=1), sq3v[:, :, 1:2])
    k.store(lambda s: s.dma_start(
        out=Gp[0:1, :].rearrange("o (c s) -> (o c) s", c=64), in_=gplane[:, 0:CH]))
    k.store(lambda s: s.dma_start(
        out=Gp[1:2, :].rearrange("o (c s) -> (o c) s", c=64), in_=gplane[:, CH:2 * CH]))
    k.finish()
    return k




def build_mid():
    k = K("mid")
    xn_in = k.din("xn", [D, NTOK])
    Gp = k.din("Gp", [2, NTOK])
    gam_in = k.din("gamma", [1, NTOK])
    wbk = k.din("wbk", [8, 1, 128, 128])
    bbk = k.din("bbk", [128, 8])
    ln2gb = k.din("ln2gb", [1, 2 * D])
    wqkv = k.din("wqkv", [12, 8, 128, 128])
    bqkv = k.din("bqkv", [128, 12])
    ltri_in = k.din("ltri", [128, 128])
    a1_out = k.dout("a1", [D, NTOK])
    heb_out = k.dout("heb", [512, NTOK])
    k.start()
    C = emit_consts(k)
    zrow = k.sb("c_zrow", [1, 128], F32)
    k.gp(lambda s: s.memset(zrow[:], 0.0))

    xng = k.sb("xng", [128, 8 * TG], F32)
    sq = k.sb("sq", [128, TG], F32)
    scr = k.sb("scr", [128, TG], F32)
    xn2b = k.sb("xn2b", [128, 8 * NTOK], BF16)
    grow = k.sb("g_row", [1, NTOK], F32)
    gv = k.sb("gv", [2, NTOK], F32)
    g2 = k.sb("g2", [1, 2 * D], F32)
    wb = k.sb("wb", [128, 8 * 128], F32)
    bbkc = k.sb("bbkc", [128, 8], F32)
    wq = k.sb("wq", [128, 12 * 8 * 128], BF16)
    bqc = k.sb("bqc", [128, 12], F32)
    ltri = k.sb("sb_ltri", [128, 128], BF16)

    k.sdma(lambda s: s.dma_start(out=gv[:], in_=Gp[:]))
    k.sdma(lambda s: s.dma_start(out=grow[:], in_=gam_in[:]))
    k.sdma(lambda s: s.dma_start(out=g2[:], in_=ln2gb[:]))
    k.sdma(lambda s: s.dma_start(out=wb.ap().rearrange("p (mb m) -> p mb m", mb=8),
                                 in_=wbk.rearrange("mb o p m -> p (mb o) m")))
    k.sdma(lambda s: s.dma_start(out=bbkc[:], in_=bbk[:]))
    k.gdma(lambda s: s.dma_start(
        out=wq.ap().rearrange("p (mb kc m) -> p mb kc m", mb=12, kc=8),
        in_=wqkv.rearrange("mb kc p m -> p mb kc m")))
    k.sdma(lambda s: s.dma_start(out=bqc[:], in_=bqkv[:]))
    k.gdma(lambda s: s.dma_start(out=ltri[:], in_=ltri_in[:]))

    def wqt(mb, kc):
        base = (mb * 8 + kc) * 128
        return wq[:, base:base + 128]

    ln2 = LN(k, "ln2", C)
    xcols = lambda kc: xng[:, kc * TG:(kc + 1) * TG]
    acols = xcols

    for t0 in range(0, NTOK, TG):
        for kc in range(8):
            k.sdma(lambda s, kc=kc, t0=t0: s.dma_start(
                out=xng[:, kc * TG:(kc + 1) * TG],
                in_=xn_in[kc * 128:(kc + 1) * 128, t0:t0 + TG]))
        for mb in range(8):
            ps = k.psum()
            k.pe(lambda t, mb=mb, t0=t0, ps=ps: t.matmul(
                out=ps[0:128, 0:TG], lhsT=wb[0:2, mb * 128:(mb + 1) * 128],
                rhs=gv[0:2, t0:t0 + TG], start=True, stop=True))
            k.ts(scr[:, 0:TG], ps[0:128, 0:TG], bbkc[:, mb:mb + 1], ALU.add)
            k.tt(acols(mb), xcols(mb), scr[:, 0:TG], ALU.add)
            k.store(lambda s, mb=mb, t0=t0: s.dma_start(
                out=a1_out[mb * 128:(mb + 1) * 128, t0:t0 + TG], in_=acols(mb)))
        ln2.stats_tg(acols, TG, sq)
        ln2.apply_tg(g2, TG, ln_consume_to(
            k, acols, scr,
            lambda kc, t0=t0: xn2b[:, kc * NTOK + t0:kc * NTOK + t0 + TG]))

    CHB = 128
    rowbuf = k.sb("rowbuf", [1, 6 * CHB], F32)
    dec_r = rowbuf[:, 0:CHB]
    dt_r = rowbuf[:, CHB:2 * CHB]
    rdt_r = rowbuf[:, 2 * CHB:3 * CHB]
    kk_r = rowbuf[:, 3 * CHB:4 * CHB]
    et_r = rowbuf[:, 4 * CHB:5 * CHB]
    zch = rowbuf[:, 5 * CHB:6 * CHB]
    k.gp(lambda s: s.memset(zch, 0.0))
    qraw = k.sb("qraw", [64, 8 * CHB], BF16)
    qp = k.sb("qp", [64, 8 * CHB], BF16)
    kraw = k.sb("kraw", [64, 8 * CHB], BF16)
    kk_tm = k.sb("kk_tm", [128, 512], BF16)
    v_tm = k.sb("v_tm", [128, 512], BF16)
    dmt = k.sb("dmt", [128, CHB], F32)
    s1 = k.sb("s1", [128, CHB], BF16)
    hebt = k.sb("hebt", [64, 8 * CHB], F32)
    W0 = k.sb("W0", [64, 512], F32)
    W1 = k.sb("W1", [64, 512], F32)
    Wbf = k.sb("Wbf", [64, 512], BF16)
    kkcol = k.sb("kkcol", [128, 1], F32)
    djcol = k.sb("djcol", [64, 1], F32)
    evs = k.sb("evs", [128, CHB], F32)
    dbc_sb = k.sb("dbc_sb", [128, CHB], F32)
    bias_bc = k.sb("bias_bc", [128, 8 * 128], F32)
    k.gp(lambda s: s.memset(W0[:], 0.0))
    Ws = [W0, W1]

    bqrow = k.din("bqkv_row", [1, 8 * 128])
    bqr = k.sb("bqr", [1, 8 * 128], F32)
    k.sdma(lambda s: s.dma_start(out=bqr[:], in_=bqrow[:]))
    for i in range(8):
        ps = k.psum()
        k.pe(lambda t, i=i, ps=ps: t.matmul(
            out=ps[0:128, 0:128], lhsT=C.row[:, 0:128],
            rhs=bqr[:, i * 128:(i + 1) * 128], start=True, stop=True))
        k.cp(bias_bc[:, i * 128:(i + 1) * 128], ps[0:128, 0:128])

    for j in range(32):
        c0 = j * CHB
        Wc = Ws[j % 2]
        Wn = Ws[(j + 1) % 2]
        k.actf(dec_r, grow[:, c0:c0 + CHB], AF.Exp, scale=-DT_HEB)
        k.dve(lambda v: v.tensor_tensor_scan(out=dt_r, data0=dec_r, data1=zch,
                                             initial=1.0, op0=ALU.mult, op1=ALU.add))
        k.recip(rdt_r, dt_r)
        k.ts(kk_r, rdt_r, dt_r[:, CHB - 1:CHB], ALU.mult)
        k.ts(kk_r, kk_r, ETA, ALU.mult)
        k.ts(et_r, dt_r, ETA, ALU.mult)
        ps_kc = k.psum()
        k.pe(lambda t, ps=ps_kc: t.matmul(out=ps[0:128, 0:1], lhsT=kk_r,
                                          rhs=C.one[:, 0:1], start=True, stop=True))
        k.cp(kkcol[:], ps_kc[0:128, 0:1])
        ps_dj = k.psum()
        k.pe(lambda t, ps=ps_dj: t.matmul(out=ps[0:64, 0:1], lhsT=C.row[:, 0:64],
                                          rhs=dt_r[:, CHB - 1:CHB], start=True,
                                          stop=True))
        k.cp(djcol[:], ps_dj[0:64, 0:1])
        ps_dbc = k.psum()
        k.pe(lambda t, ps=ps_dbc: t.matmul(out=ps[0:128, 0:CHB], lhsT=C.row[:, 0:128],
                                           rhs=dt_r, start=True, stop=True))
        k.cp(dbc_sb[:], ps_dbc[0:128, 0:CHB])
        ps_o = k.psum()
        k.pe(lambda t, ps=ps_o: t.matmul(out=ps[0:128, 0:CHB], lhsT=rdt_r,
                                         rhs=et_r, start=True, stop=True))
        k.tt(dmt[:], ps_o[0:128, 0:CHB], ltri[:], ALU.mult)

        x2cols = lambda kc, c0=c0: xn2b[:, kc * NTOK + c0:kc * NTOK + c0 + CHB]
        for h in range(8):
            mb, off = h // 2, 64 * (h % 2)
            ps = k.psum()
            def fq(t, mb=mb, off=off, ps=ps):
                last = None
                for kc in range(8):
                    last = t.matmul(out=ps[0:64, 0:CHB],
                                    lhsT=wqt(mb, kc)[:, off:off + 64],
                                    rhs=x2cols(kc), start=(kc == 0), stop=(kc == 7))
                return last
            k.pe(fq)
            k.ts(evs[0:64, 0:CHB], ps[0:64, 0:CHB], bqc[off:off + 64, mb:mb + 1],
                 ALU.add)
            k.cp(qraw[:, h * CHB:(h + 1) * CHB], evs[0:64, 0:CHB])
            k.tt(qp[:, h * CHB:(h + 1) * CHB], evs[0:64, 0:CHB],
                 dbc_sb[0:64, 0:CHB], ALU.mult)
        for h in range(8):
            mb, off = 4 + h // 2, 64 * (h % 2)
            ps = k.psum()
            def fk(t, mb=mb, off=off, ps=ps):
                last = None
                for kc in range(8):
                    last = t.matmul(out=ps[0:64, 0:CHB],
                                    lhsT=wqt(mb, kc)[:, off:off + 64],
                                    rhs=x2cols(kc), start=(kc == 0), stop=(kc == 7))
                return last
            k.pe(fk)
            k.ts(kraw[:, h * CHB:(h + 1) * CHB], ps[0:64, 0:CHB],
                 bqc[off:off + 64, mb:mb + 1], ALU.add)
        for mb in range(4, 12):
            ps = k.psum()
            def fkv(t, mb=mb, ps=ps):
                last = None
                for kc in range(8):
                    last = t.matmul(out=ps[0:128, 0:128], lhsT=x2cols(kc),
                                    rhs=wqt(mb, kc), start=(kc == 0), stop=(kc == 7))
                return last
            k.pe(fkv)
            bb = bias_bc[:, (mb - 4) * 128:(mb - 3) * 128]
            if mb < 8:
                co = (mb - 4) * 128
                k.tt(evs[:], ps[0:128, 0:128], bb, ALU.add)
                k.ts(kk_tm[:, co:co + 128], evs[:], kkcol[:, 0:1], ALU.mult)
            else:
                co = (mb - 8) * 128
                k.tt(v_tm[:, co:co + 128], ps[0:128, 0:128], bb, ALU.add)

        k.cp(Wbf[:], Wc[:])
        ps_u = k.psum_pin()
        for h in range(8):
            tile = h // 2
            off = 64 * (h % 2)
            ps_s = k.psum()
            k.pe(lambda t, h=h, ps=ps_s: t.matmul(
                out=ps[0:128, 0:CHB],
                lhsT=kraw[:, h * CHB:(h + 1) * CHB],
                rhs=qraw[:, h * CHB:(h + 1) * CHB],
                start=True, stop=True))
            k.tt(s1[:], ps_s[0:128, 0:CHB], dmt[:], ALU.mult)
            ps_av = k.psum()
            def fav(t, h=h, ps=ps_av):
                t.matmul(out=ps[0:64, 0:CHB], lhsT=v_tm[:, h * 64:(h + 1) * 64],
                         rhs=s1[:], start=True, stop=False)
                return t.matmul(out=ps[0:64, 0:CHB],
                                lhsT=Wbf[:, h * 64:(h + 1) * 64],
                                rhs=qp[:, h * CHB:(h + 1) * CHB],
                                start=False, stop=True)
            k.pe(fav)
            k.cp(hebt[:, h * CHB:(h + 1) * CHB], ps_av[0:64, 0:CHB])
            k.pe(lambda t, h=h, ps=ps_u: t.matmul(
                out=ps[0:64, h * 64:(h + 1) * 64],
                lhsT=kk_tm[:, h * 64:(h + 1) * 64],
                rhs=v_tm[:, h * 64:(h + 1) * 64],
                start=True, stop=True))
        k.dve(lambda v, Wn=Wn, Wc=Wc, ps=ps_u: v.scalar_tensor_tensor(
            out=Wn[:], in0=Wc[:], scalar=djcol[:, 0:1], in1=ps[0:64, 0:512],
            op0=ALU.mult, op1=ALU.add))
        for h in range(8):
            k.store(lambda s, h=h, c0=c0: s.dma_start(
                out=heb_out[h * 64:(h + 1) * 64, c0:c0 + CHB],
                in_=hebt[:, h * CHB:(h + 1) * CHB]))
    k.finish()
    return k



def build_a0end():
    k = K("a0end")
    T = TOWN
    a1_in = k.din("a1", [D, T])
    heb_in = k.din("heb", [512, T])
    wout = k.din("wout", [8, 4, 128, 128])
    bout = k.din("bout", [128, 8])
    ln3gb = k.din("ln3gb", [1, 2 * D])
    wff1 = k.din("wff1", [32, 8, 128, 128])
    bff1 = k.din("bff1", [128, 32])
    wff2 = k.din("wff2", [8, 32, 128, 128])
    bff2 = k.din("bff2", [128, 8])
    xA_out = k.dout("xA", [D, T])
    k.start()
    C = emit_consts(k)

    a1g = k.sb("a1g", [128, 8 * TG], F32)
    hebg = k.sb("hebg", [128, 4 * TG], BF16)
    x3b = k.sb("x3b", [128, 8 * TG], BF16)
    hid = k.sb("hid", [128, 32 * TG], BF16)
    sq = k.sb("sq", [128, TG], F32)
    scr = k.sb("scr", [128, TG], F32)
    wo = k.sb("wo", [128, 8 * 4 * 128], BF16)
    w1b = k.sb("w1b", [128, 8 * 128], BF16)
    w2b = k.sb("w2b", [128, 32 * 128], BF16)
    boutc = k.sb("boutc", [128, 8], F32)
    b1c = k.sb("b1c", [128, 32], F32)
    b2c = k.sb("b2c", [128, 8], F32)
    g3 = k.sb("g3", [1, 2 * D], F32)

    k.gdma(lambda s: s.dma_start(
        out=wo.ap().rearrange("p (mb kc m) -> p mb kc m", mb=8, kc=4),
        in_=wout.rearrange("mb kc p m -> p mb kc m")))
    k.sdma(lambda s: s.dma_start(out=boutc[:], in_=bout[:]))
    k.sdma(lambda s: s.dma_start(out=b1c[:], in_=bff1[:]))
    k.sdma(lambda s: s.dma_start(out=b2c[:], in_=bff2[:]))
    k.sdma(lambda s: s.dma_start(out=g3[:], in_=ln3gb[:]))

    ln3 = LN(k, "ln3", C)
    acols = lambda kc: a1g[:, kc * TG:(kc + 1) * TG]

    for t0 in range(0, T, TG):
        for kc in range(8):
            k.sdma(lambda s, kc=kc, t0=t0: s.dma_start(
                out=a1g[:, kc * TG:(kc + 1) * TG],
                in_=a1_in[kc * 128:(kc + 1) * 128, t0:t0 + TG]))
        for kc in range(4):
            k.gdma(lambda s, kc=kc, t0=t0: s.dma_start(
                out=hebg[:, kc * TG:(kc + 1) * TG],
                in_=heb_in[kc * 128:(kc + 1) * 128, t0:t0 + TG]))
        for mb in range(8):
            ps = k.psum()
            def fo(t, mb=mb, ps=ps):
                last = None
                for kc in range(4):
                    last = t.matmul(out=ps[0:128, 0:TG],
                                    lhsT=wo[:, (mb * 4 + kc) * 128:(mb * 4 + kc + 1) * 128],
                                    rhs=hebg[:, kc * TG:(kc + 1) * TG],
                                    start=(kc == 0), stop=(kc == 3))
                return last
            k.pe(fo)
            k.ts(scr[:, 0:TG], ps[0:128, 0:TG], boutc[:, mb:mb + 1], ALU.add)
            k.tt(acols(mb), acols(mb), scr[:, 0:TG], ALU.add)
        ln3.stats_tg(acols, TG, sq)
        ln3.apply_tg(g3, TG, ln_consume_to(
            k, acols, scr, lambda kc: x3b[:, kc * TG:(kc + 1) * TG]))
        for mb in range(32):
            k.gdma(lambda s, mb=mb: s.dma_start(
                out=w1b.ap().rearrange("p (kc m) -> p kc m", kc=8),
                in_=wff1[mb].rearrange("kc p m -> p kc m")))
            ps = k.psum()
            def f1(t, mb=mb, ps=ps):
                last = None
                for kc in range(8):
                    last = t.matmul(out=ps[0:128, 0:TG],
                                    lhsT=w1b[:, kc * 128:(kc + 1) * 128],
                                    rhs=x3b[:, kc * TG:(kc + 1) * TG],
                                    start=(kc == 0), stop=(kc == 7))
                return last
            k.pe(f1)
            k.actf(hid[:, mb * TG:(mb + 1) * TG], ps[0:128, 0:TG],
                   AF.Gelu_apprx_tanh, bias=b1c[:, mb:mb + 1])
        for mb in range(8):
            k.gdma(lambda s, mb=mb: s.dma_start(
                out=w2b.ap().rearrange("p (kc m) -> p kc m", kc=32),
                in_=wff2[mb].rearrange("kc p m -> p kc m")))
            ps = k.psum()
            def f2(t, mb=mb, ps=ps):
                last = None
                for kc in range(32):
                    last = t.matmul(out=ps[0:128, 0:TG],
                                    lhsT=w2b[:, kc * 128:(kc + 1) * 128],
                                    rhs=hid[:, kc * TG:(kc + 1) * TG],
                                    start=(kc == 0), stop=(kc == 31))
                return last
            k.pe(f2)
            k.ts(scr[:, 0:TG], ps[0:128, 0:TG], b2c[:, mb:mb + 1], ALU.add)
            k.tt(scr[:, 0:TG], scr[:, 0:TG], acols(mb), ALU.add)
            k.store(lambda s, mb=mb, t0=t0: s.dma_start(
                out=xA_out[mb * 128:(mb + 1) * 128, t0:t0 + TG], in_=scr[:, 0:TG]))
    k.finish()
    return k



def build_ffn_attn():
    k = K("ffn_attn")
    T = TOWN
    TK = T + 512
    NTT = TK // 128
    xin = k.din("xin", [D, T])
    xhal = k.din("xhal", [D, 513])
    wqkv = k.din("wqkv", [12, 8, 128, 128])
    wo8 = k.din("wo8", [8, 8, 64, 128])
    qscol = k.din("qscol", [64, 1])
    kscol = k.din("kscol", [64, 1])
    t0in = k.din("t0", [1, 1])
    pfin = k.din("pflag", [1, 1])
    rotin = k.din("rot64", [64, 64])
    ymid_out = k.dout("ymid", [D, T])
    dbg_qn = k.dout("dbg_qn", [64, 8 * T], mybir.dt.bfloat16)
    dbg_kn = k.dout("dbg_kn", [64, 8 * (T + 512)], mybir.dt.bfloat16)
    dbg_aq = k.dout("dbg_aq", [64, 2 * T], mybir.dt.bfloat16)
    dbg_m = k.dout("dbg_m", [128, 8])
    dbg_ofm = k.dout("dbg_ofm", [64, 8 * TG], mybir.dt.bfloat16)
    dbg_qraw = k.dout("dbg_qraw", [64, TG])
    dbg_hsh = k.dout("dbg_hsh", [128, 8 * TG], mybir.dt.bfloat16)
    k.start()
    C = emit_consts(k)

    wq = k.sb("wq", [128, 12 * 8 * 128], BF16)
    wob = k.sb("wob", [64, 8 * 8 * 128], BF16)
    qn = k.sb("qn", [64, 8 * T], BF16)
    kn = k.sb("kn", [64, 8 * TK], BF16)
    vaug = k.sb("vaug", [128, NTT * 520], BF16)
    Aq = k.sb("Aq", [64, T], BF16)
    Bq = k.sb("Bq", [64, T], BF16)
    Ak = k.sb("Ak", [64, TK], BF16)
    Bk = k.sb("Bk", [64, TK], BF16)
    msk = k.sb("msk", [128, 12 * TG], BF16)
    ofm = k.sb("ofm", [64, 8 * TG], BF16)
    qsc = k.sb("qsc", [64, 1], F32)
    ksc = k.sb("ksc", [64, 1], F32)
    rot = k.sb("rot", [64, 64], F32)
    t0c = k.sb("t0c", [1, 1], F32)
    pfs = k.sb("pfs", [1, 1], F32)
    pfc = k.sb("pfc", [128, 1], F32)
    hshg = k.sb("hshg", [128, 8 * TG], BF16)
    evs = k.sb("evs", [64, TG], F32)
    ev2 = k.sb("ev2", [64, TG], F32)
    ev3 = k.sb("ev3", [64, TG], F32)
    bv2 = k.sb("bv2", [64, TG], BF16)
    bv3 = k.sb("bv3", [64, TG], BF16)
    sqg = k.sb("sqg", [64, TG], F32)
    rowz = k.sb("rowz", [1, 3 * TG], F32)
    mrow = k.sb("mrow", [1, 16], F32)
    negm8 = k.sb("negm8", [128, 8], F32)
    pt = k.sb("pt", [128, TG], BF16)
    sfx = k.sb("sfx", [128, TG], F32)
    rrow = k.sb("rrow", [1, TG], F32)
    rbt = k.sb("rbt", [64, TG], F32)
    xg = k.sb("xg", [128, TG], F32)
    mtmp = xg
    jq = k.sb("jq", [128, 1], F32)
    invf = k.sb("invf", [1, 64], F32)
    lnb = k.sb("lnb", [1, 64], F32)

    k.gdma(lambda s: s.dma_start(
        out=wq.ap().rearrange("p (mb kc m) -> p mb kc m", mb=12, kc=8),
        in_=wqkv.rearrange("mb kc p m -> p mb kc m")))
    k.gdma(lambda s: s.dma_start(
        out=wob.ap().rearrange("p (mb hc m) -> p mb hc m", mb=8, hc=8),
        in_=wo8.rearrange("mb hc p m -> p mb hc m")))
    k.sdma(lambda s: s.dma_start(out=qsc[:], in_=qscol[:]))
    k.sdma(lambda s: s.dma_start(out=ksc[:], in_=kscol[:]))
    k.sdma(lambda s: s.dma_start(out=rot[:], in_=rotin[:]))
    k.sdma(lambda s: s.dma_start(out=t0c[:], in_=t0in[:]))
    k.sdma(lambda s: s.dma_start(out=pfs[:], in_=pfin[:]))
    ps_pf = k.psum()
    k.pe(lambda t, ps=ps_pf: t.matmul(out=ps[0:128, 0:1], lhsT=C.row[:, 0:128],
                                      rhs=pfs[:], start=True, stop=True))
    k.cp(pfc[:], ps_pf[0:128, 0:1])

    def wqt(mb, kc):
        base = (mb * 8 + kc) * 128
        return wq[:, base:base + 128]

    def wot(mb, hc):
        base = (mb * 8 + hc) * 128
        return wob[:, base:base + 128]

    k.gp(lambda s: s.iota(out=jq[:], pattern=[[1, 1]], base=0, channel_multiplier=1,
                          allow_small_or_imprecise_dtypes=True))
    for kt in range(4):
        k.gp(lambda s: s.iota(out=mtmp[:], pattern=[[1, TG]], base=0,
                              channel_multiplier=0,
                              allow_small_or_imprecise_dtypes=True))
        k.ts(mtmp[:], mtmp[:], jq[:, 0:1], ALU.subtract)
        k.ts(mtmp[:], mtmp[:], float(-128 * kt) + 1.0, ALU.add)
        k.ts(mtmp[:], mtmp[:], 0.0, ALU.max)
        k.ts(mtmp[:], mtmp[:], 1.0, ALU.min)
        k.cp(msk[:, (4 + kt) * TG:(5 + kt) * TG], mtmp[:])
        k.gp(lambda s: s.iota(out=mtmp[:], pattern=[[1, TG]], base=0,
                              channel_multiplier=0,
                              allow_small_or_imprecise_dtypes=True))
        k.ts(mtmp[:], mtmp[:], -1.0, ALU.mult)
        k.ts(mtmp[:], mtmp[:], jq[:, 0:1], ALU.add)
        k.ts(mtmp[:], mtmp[:], float(128 * kt) + 1.0, ALU.add)
        k.ts(mtmp[:], mtmp[:], 0.0, ALU.max)
        k.ts(mtmp[:], mtmp[:], 1.0, ALU.min)
        k.cp(msk[:, kt * TG:(kt + 1) * TG], mtmp[:])
        k.ts(mtmp[:], mtmp[:], pfc[:, 0:1], ALU.mult)
        k.cp(msk[:, (8 + kt) * TG:(9 + kt) * TG], mtmp[:])

    k.gp(lambda s: s.iota(out=invf[:], pattern=[[1, 64]], base=0,
                          channel_multiplier=0, allow_small_or_imprecise_dtypes=True))
    k.ts(lnb[:], invf[:], -31.0, ALU.add)
    k.ts(lnb[:], lnb[:], 0.0, ALU.max)
    k.ts(lnb[:], lnb[:], 1.0, ALU.min)
    k.ts(lnb[:], lnb[:], -32.0, ALU.mult)
    k.tt(invf[:], invf[:], lnb[:], ALU.add)
    k.cp(lnb[:], invf[:])
    k.ts(invf[:], invf[:], -math.log(10000.0) / 32.0, ALU.mult)
    k.actf(invf[:], invf[:], AF.Exp)
    k.ts(lnb[:], lnb[:], 2.0 / (1.4 * 64.0), ALU.mult)
    k.ts(lnb[:], lnb[:], 0.4 / 1.4, ALU.add)
    k.actf(lnb[:], lnb[:], AF.Ln)
    irow = rowz[:, 0:TG]

    def build_tables(Adst, Bdst, Trange, toff, sgn):
        for g0 in range(0, Trange, TG):
            k.gp(lambda s: s.iota(out=irow, pattern=[[1, TG]], base=0,
                                  channel_multiplier=0,
                                  allow_small_or_imprecise_dtypes=True))
            k.ts(irow, irow, float(g0 + toff), ALU.add)
            k.ts(irow, irow, t0c[0:1, 0:1], ALU.add)
            ps_th = k.psum()
            k.pe(lambda t, ps=ps_th: t.matmul(out=ps[0:64, 0:TG], lhsT=invf[:],
                                              rhs=irow, start=True, stop=True))
            MAGIC = 12582912.0
            k.ts(evs[:], ps_th[0:64, 0:TG], 1.0 / (2.0 * math.pi), ALU.mult)
            k.ts(ev2[:], evs[:], MAGIC, ALU.add)
            k.ts(ev2[:], ev2[:], -MAGIC, ALU.add)
            k.tt(ev3[:], evs[:], ev2[:], ALU.subtract)
            k.actf(ev2[:], ev3[:], AF.Sin, scale=2.0 * math.pi)
            k.ts(evs[:], evs[:], 0.25, ALU.add)
            k.ts(ev3[:], evs[:], MAGIC, ALU.add)
            k.ts(ev3[:], ev3[:], -MAGIC, ALU.add)
            k.tt(evs[:], evs[:], ev3[:], ALU.subtract)
            k.actf(ev3[:], evs[:], AF.Sin, scale=2.0 * math.pi)
            k.gp(lambda s: s.iota(out=irow, pattern=[[1, TG]], base=0,
                                  channel_multiplier=0,
                                  allow_small_or_imprecise_dtypes=True))
            k.ts(irow, irow, float(g0 + toff - 2048), ALU.add)
            k.ts(irow, irow, t0c[0:1, 0:1], ALU.add)
            k.ts(irow, irow, sgn / 256.0, ALU.mult)
            ps_l = k.psum()
            k.pe(lambda t, ps=ps_l: t.matmul(out=ps[0:64, 0:TG], lhsT=lnb[:],
                                             rhs=irow, start=True, stop=True))
            k.actf(evs[:], ps_l[0:64, 0:TG], AF.Exp)
            k.tt(Adst[:, g0:g0 + TG], ev3[:], evs[:], ALU.mult)
            k.tt(Bdst[:, g0:g0 + TG], ev2[:], evs[:], ALU.mult)

    build_tables(Aq, Bq, T, 0, 1.0)
    build_tables(Ak, Bk, TK, -512, -1.0)

    k.gp(lambda s: s.memset(vaug[:], 1.0))
    k.gp(lambda s: s.memset(mrow[:], 0.0))

    def norm_rot_store(ps_q, sccol, Atab, Btab, dst, dcol, mcol):
        k.cp(evs[:], ps_q[0:64, 0:TG])
        k.actf(sqg[:], evs[:], AF.Square)
        ps_n = k.psum()
        k.pe(lambda t, ps=ps_n: t.matmul(out=ps[0:1, 0:TG], lhsT=C.col[0:64, 0:1],
                                         rhs=sqg[:], start=True, stop=True))
        nr = rowz[:, TG:2 * TG]
        k.actf(nr, ps_n[0:1, 0:TG], AF.Sqrt)
        k.ts(nr, nr, 1e-12, ALU.max)
        k.recip(nr, nr)
        ps_b = k.psum()
        k.pe(lambda t, ps=ps_b: t.matmul(out=ps[0:64, 0:TG], lhsT=C.row[:, 0:64],
                                         rhs=nr, start=True, stop=True))
        k.cp(rbt[:], ps_b[0:64, 0:TG])
        k.tt(ev2[:], evs[:], rbt[:], ALU.mult)
        k.ts(ev2[:], ev2[:], sccol, ALU.mult)
        k.tt(sqg[:], ev2[:], ev2[:], ALU.mult)
        ps_m = k.psum()
        k.pe(lambda t, ps=ps_m: t.matmul(out=ps[0:1, 0:TG], lhsT=C.col[0:64, 0:1],
                                         rhs=sqg[:], start=True, stop=True))
        mr = rowz[:, 2 * TG:2 * TG + 1]
        k.dve(lambda v: v.reduce_max(out=mr, in_=ps_m[0:1, 0:TG], axis=AX.X))
        k.tt(mrow[:, mcol:mcol + 1], mrow[:, mcol:mcol + 1], mr, ALU.max)
        ps_r = k.psum()
        k.pe(lambda t, ps=ps_r: t.matmul(out=ps[0:64, 0:TG], lhsT=rot[:],
                                         rhs=ev2[:], start=True, stop=True))
        k.cp(ev3[:], ps_r[0:64, 0:TG])
        k.cp(bv2[:], ev2[:])
        k.cp(bv3[:], ev3[:])
        k.tt(bv2[:], bv2[:], Atab, ALU.mult)
        k.tt(bv3[:], bv3[:], Btab, ALU.mult)
        k.tt(dst[:, dcol:dcol + TG], bv2[:], bv3[:], ALU.add)

    for g in range(5):
        g0 = g * TG
        for kc in range(8):
            src = kc
            shift = 0 if kc < 4 else -1
            a_ = g0 + shift
            dst = hshg[:, kc * TG:(kc + 1) * TG]
            n_hal = max(0, min(512 - a_, TG))
            if n_hal > 0:
                k.gdma(lambda s, src=src, a_=a_, n=n_hal, dst=dst: s.dma_start(
                    out=dst[:, 0:n],
                    in_=xhal[src * 128:(src + 1) * 128, a_ + 1:a_ + 1 + n],
                    allow_slow_non_contiguous=(n < 8)))
            if n_hal < TG:
                o0 = max(0, 512 - a_)
                k.gdma(lambda s, src=src, a_=a_, o0=o0, dst=dst: s.dma_start(
                    out=dst[:, o0:TG],
                    in_=xin[src * 128:(src + 1) * 128, a_ + o0 - 512:a_ + TG - 512]))
        if g == 1:
            k.store(lambda s: s.dma_start(out=dbg_hsh[:], in_=hshg[:]))
        for h in range(8):
            mbq, off = h // 2, 64 * (h % 2)
            if g >= 1:
                tq = g0 - 512
                ps_q = k.psum()
                def fq(t, mbq=mbq, off=off, ps=ps_q):
                    last = None
                    for kc in range(8):
                        last = t.matmul(out=ps[0:64, 0:TG],
                                        lhsT=wqt(mbq, kc)[:, off:off + 64],
                                        rhs=hshg[:, kc * TG:(kc + 1) * TG],
                                        start=(kc == 0), stop=(kc == 7))
                    return last
                k.pe(fq)
                if g == 1 and h == 0:
                    k.cp(evs[:], ps_q[0:64, 0:TG])
                    k.store(lambda s: s.dma_start(out=dbg_qraw[:], in_=evs[:]))
                norm_rot_store(ps_q, qsc[:, 0:1], Aq[:, tq:tq + TG], Bq[:, tq:tq + TG],
                               qn, h * T + tq, h)
            ps_k = k.psum()
            def fk(t, mbq=mbq, off=off, ps=ps_k):
                last = None
                for kc in range(8):
                    last = t.matmul(out=ps[0:64, 0:TG],
                                    lhsT=wqt(4 + mbq, kc)[:, off:off + 64],
                                    rhs=hshg[:, kc * TG:(kc + 1) * TG],
                                    start=(kc == 0), stop=(kc == 7))
                return last
            k.pe(fk)
            norm_rot_store(ps_k, ksc[:, 0:1], Ak[:, g0:g0 + TG], Bk[:, g0:g0 + TG],
                           kn, h * TK + g0, 8 + h)
        for tt in range(4):
            for mb in range(8, 12):
                ps_v = k.psum()
                def fv(t, tt=tt, mb=mb, ps=ps_v):
                    last = None
                    for kc in range(8):
                        last = t.matmul(
                            out=ps[0:128, 0:128],
                            lhsT=hshg[:, kc * TG + tt * 128:kc * TG + (tt + 1) * 128],
                            rhs=wqt(mb, kc), start=(kc == 0), stop=(kc == 7))
                    return last
                k.pe(fv)
                tta = g * 4 + tt
                h0 = 2 * (mb - 8)
                k.cp(vaug[:, tta * 520 + h0 * 65:tta * 520 + h0 * 65 + 64],
                     ps_v[0:128, 0:64])
                k.cp(vaug[:, tta * 520 + (h0 + 1) * 65:tta * 520 + (h0 + 1) * 65 + 64],
                     ps_v[0:128, 64:128])

    for h in range(8):
        mm = rowz[:, 2 * TG + 2:2 * TG + 3]
        k.tt(mm, mrow[:, h:h + 1], mrow[:, 8 + h:9 + h], ALU.mult)
        k.actf(mm, mm, AF.Sqrt)
        ps_nm = k.psum()
        k.pe(lambda t, ps=ps_nm: t.matmul(out=ps[0:128, 0:1], lhsT=C.row[:, 0:128],
                                          rhs=mm, start=True, stop=True))
        k.cp(negm8[:, h:h + 1], ps_nm[0:128, 0:1])

    k.store(lambda s: s.dma_start(out=dbg_qn[:], in_=qn[:]))
    k.store(lambda s: s.dma_start(out=dbg_kn[:], in_=kn[:]))
    k.store(lambda s: s.dma_start(out=dbg_aq[:, 0:T], in_=Aq[:]))
    k.store(lambda s: s.dma_start(out=dbg_aq[:, T:2 * T], in_=Bq[:]))
    k.store(lambda s: s.dma_start(out=dbg_m[:], in_=negm8[:]))
    for qw in range(4):
        for h in range(8):
            ps_av = k.psum_pin()
            for kt in range(8):
                ps_s = k.psum()
                kcol = h * TK + qw * 512 + kt * 128
                k.pe(lambda t, kcol=kcol, h=h, qw=qw, ps=ps_s: t.matmul(
                    out=ps[0:128, 0:TG], lhsT=kn[:, kcol:kcol + 128],
                    rhs=qn[:, h * T + qw * TG:h * T + (qw + 1) * TG],
                    start=True, stop=True))
                k.ts(sfx[:], ps_s[0:128, 0:TG], negm8[:, h:h + 1], ALU.subtract)
                k.ts(sfx[:], sfx[:], 2.5, ALU.min)
                k.actf(pt[:], sfx[:], AF.Exp, scale=8.0)
                if kt < 4:
                    mi = (8 + kt) if qw == 0 else kt
                else:
                    mi = kt
                k.tt(pt[:], pt[:], msk[:, mi * TG:(mi + 1) * TG], ALU.mult)
                tta = qw * 4 + kt
                k.pe(lambda t, tta=tta, h=h, ps=ps_av, kt=kt: t.matmul(
                    out=ps[0:65, 0:TG],
                    lhsT=vaug[:, tta * 520 + h * 65:tta * 520 + (h + 1) * 65],
                    rhs=pt[:], start=(kt == 0), stop=(kt == 7)))
            k.recip(rrow[:], ps_av[64:65, 0:TG])
            ps_rb = k.psum()
            k.pe(lambda t, ps=ps_rb: t.matmul(out=ps[0:64, 0:TG], lhsT=C.row[:, 0:64],
                                              rhs=rrow[:], start=True, stop=True))
            k.cp(rbt[:], ps_rb[0:64, 0:TG])
            k.cp(evs[:], ps_av[0:64, 0:TG])
            k.tt(ofm[:, h * TG:(h + 1) * TG], evs[:], rbt[:], ALU.mult)
        if qw == 0:
            k.store(lambda s: s.dma_start(out=dbg_ofm[:], in_=ofm[:]))
        for mb in range(8):
            ps_w = k.psum()
            def fw(t, mb=mb, ps=ps_w):
                last = None
                for hc in range(8):
                    last = t.matmul(out=ps[0:128, 0:TG], lhsT=wot(mb, hc),
                                    rhs=ofm[:, hc * TG:(hc + 1) * TG],
                                    start=(hc == 0), stop=(hc == 7))
                return last
            k.pe(fw)
            k.sdma(lambda s, mb=mb, qw=qw: s.dma_start(
                out=xg[:], in_=xin[mb * 128:(mb + 1) * 128, qw * TG:(qw + 1) * TG]))
            k.tt(xg[:], xg[:], ps_w[0:128, 0:TG], ALU.add)
            k.store(lambda s, mb=mb, qw=qw: s.dma_start(
                out=ymid_out[mb * 128:(mb + 1) * 128, qw * TG:(qw + 1) * TG],
                in_=xg[:]))
    k.finish()
    return k



def build_ffn_ffn():
    k = K("ffn_ffn")
    T = TOWN
    NHB = 22
    ymid_in = k.din("ymid", [D, T])
    prevtok = k.din("prevtok", [D, 1])
    lngb = k.din("lngb", [1, 2 * D])
    w1a = k.din("w1a", [NHB, 8, 128, 128])
    w1g = k.din("w1g", [NHB, 8, 128, 128])
    b1a = k.din("b1a", [128, NHB])
    b1g = k.din("b1g", [128, NHB])
    w2 = k.din("w2", [8, NHB, 128, 128])
    b2 = k.din("b2", [128, 8])
    y_out = k.dout("y", [D, T])
    k.start()
    C = emit_consts(k)

    hf = k.sb("hf", [128, 8 * TG], F32)
    yg = k.sb("yg", [128, 8 * TG], F32)
    hnb = k.sb("hnb", [128, 8 * TG], BF16)
    ghid = k.sb("ghid", [128, NHB * TG], BF16)
    sq = k.sb("sq", [128, TG], F32)
    scr = k.sb("scr", [128, TG], F32)
    gsc = k.sb("gsc", [128, TG], F32)
    w1ab = k.sb("w1ab", [128, 8 * 128], BF16)
    w1gb = k.sb("w1gb", [128, 8 * 128], BF16)
    w2b = k.sb("w2b", [128, NHB * 128], BF16)
    b1ac = k.sb("b1ac", [128, NHB], F32)
    b1gc = k.sb("b1gc", [128, NHB], F32)
    b2c = k.sb("b2c", [128, 8], F32)
    gl = k.sb("gl", [1, 2 * D], F32)

    k.sdma(lambda s: s.dma_start(out=b1ac[:], in_=b1a[:]))
    k.sdma(lambda s: s.dma_start(out=b1gc[:], in_=b1g[:]))
    k.sdma(lambda s: s.dma_start(out=b2c[:], in_=b2[:]))
    k.sdma(lambda s: s.dma_start(out=gl[:], in_=lngb[:]))

    ln = LN(k, "ffln", C)
    hcols = lambda kc: hf[:, kc * TG:(kc + 1) * TG]
    ycols = lambda kc: yg[:, kc * TG:(kc + 1) * TG]

    for t0 in range(0, T, TG):
        for kc in range(8):
            k.sdma(lambda s, kc=kc, t0=t0: s.dma_start(
                out=yg[:, kc * TG:(kc + 1) * TG],
                in_=ymid_in[kc * 128:(kc + 1) * 128, t0:t0 + TG]))
        for kc in range(8):
            if kc < 4:
                k.sdma(lambda s, kc=kc, t0=t0: s.dma_start(
                    out=hf[:, kc * TG:(kc + 1) * TG],
                    in_=ymid_in[kc * 128:(kc + 1) * 128, t0:t0 + TG]))
            else:
                if t0 == 0:
                    k.sdma(lambda s, kc=kc: s.dma_start(
                        out=hf[:, kc * TG:kc * TG + 1],
                        in_=prevtok[kc * 128:(kc + 1) * 128, 0:1]))
                    k.sdma(lambda s, kc=kc, t0=t0: s.dma_start(
                        out=hf[:, kc * TG + 1:(kc + 1) * TG],
                        in_=ymid_in[kc * 128:(kc + 1) * 128, t0:t0 + TG - 1]))
                else:
                    k.sdma(lambda s, kc=kc, t0=t0: s.dma_start(
                        out=hf[:, kc * TG:(kc + 1) * TG],
                        in_=ymid_in[kc * 128:(kc + 1) * 128, t0 - 1:t0 + TG - 1]))
        ln.stats_tg(hcols, TG, sq)
        ln.apply_tg(gl, TG, ln_consume_to(
            k, hcols, scr, lambda kc: hnb[:, kc * TG:(kc + 1) * TG]))
        for i in range(NHB):
            k.gdma(lambda s, i=i: s.dma_start(
                out=w1ab.ap().rearrange("p (kc m) -> p kc m", kc=8),
                in_=w1a[i].rearrange("kc p m -> p kc m")))
            k.gdma(lambda s, i=i: s.dma_start(
                out=w1gb.ap().rearrange("p (kc m) -> p kc m", kc=8),
                in_=w1g[i].rearrange("kc p m -> p kc m")))
            ps_a = k.psum()
            ps_g = k.psum()
            def fa(t, ps=ps_a):
                last = None
                for kc in range(8):
                    last = t.matmul(out=ps[0:128, 0:TG],
                                    lhsT=w1ab[:, kc * 128:(kc + 1) * 128],
                                    rhs=hnb[:, kc * TG:(kc + 1) * TG],
                                    start=(kc == 0), stop=(kc == 7))
                return last
            k.pe(fa)
            def fg(t, ps=ps_g):
                last = None
                for kc in range(8):
                    last = t.matmul(out=ps[0:128, 0:TG],
                                    lhsT=w1gb[:, kc * 128:(kc + 1) * 128],
                                    rhs=hnb[:, kc * TG:(kc + 1) * TG],
                                    start=(kc == 0), stop=(kc == 7))
                return last
            k.pe(fg)
            k.actf(gsc[:], ps_g[0:128, 0:TG], AF.Gelu, bias=b1gc[:, i:i + 1])
            k.ts(scr[:, 0:TG], ps_a[0:128, 0:TG], b1ac[:, i:i + 1], ALU.add)
            k.tt(ghid[:, i * TG:(i + 1) * TG], scr[:, 0:TG], gsc[:], ALU.mult)
        for mb in range(8):
            k.gdma(lambda s, mb=mb: s.dma_start(
                out=w2b.ap().rearrange("p (kc m) -> p kc m", kc=NHB),
                in_=w2[mb].rearrange("kc p m -> p kc m")))
            ps = k.psum()
            def f2(t, mb=mb, ps=ps):
                last = None
                for kc in range(NHB):
                    last = t.matmul(out=ps[0:128, 0:TG],
                                    lhsT=w2b[:, kc * 128:(kc + 1) * 128],
                                    rhs=ghid[:, kc * TG:(kc + 1) * TG],
                                    start=(kc == 0), stop=(kc == NHB - 1))
                return last
            k.pe(f2)
            k.ts(scr[:, 0:TG], ps[0:128, 0:TG], b2c[:, mb:mb + 1], ALU.add)
            k.tt(scr[:, 0:TG], scr[:, 0:TG], ycols(mb), ALU.add)
            k.store(lambda s, mb=mb, t0=t0: s.dma_start(
                out=y_out[mb * 128:(mb + 1) * 128, t0:t0 + TG], in_=scr[:, 0:TG]))
    k.finish()
    return k



def build_comb0kv():
    k = K("comb0kv")
    T = TOWN
    x_in = k.din("x", [D, T])
    xA_in = k.din("xA", [D, T])
    xF_in = k.din("xF", [D, T])
    wkT = k.din("wkT", [8, 8, 128, 128])
    bkcol = k.din("bkcol", [128, 8])
    wvT = k.din("wvT", [8, 8, 128, 128])
    bvrow = k.din("bvrow", [1, D])
    x1_out = k.dout("x1", [D, T])
    xn1_out = k.dout("xn1", [D, T])
    K_out = k.dout("Kf", [D, T], BF16)
    V_out = k.dout("Vt", [T, D], BF16)
    k.start()
    C = emit_consts(k)

    acc = k.sb("acc", [128, 8 * TG], F32)
    tmp = k.sb("tmp", [128, TG], F32)
    sq = k.sb("sq", [128, TG], F32)
    scr = k.sb("scr", [128, TG], F32)
    x1b = k.sb("x1b", [128, 8 * TG], BF16)
    wk = k.sb("wk", [128, 8 * 8 * 128], BF16)
    wv = k.sb("wv", [128, 8 * 8 * 128], BF16)
    bkc = k.sb("bkc", [128, 8], F32)
    bvr = k.sb("bvr", [1, D], F32)
    bvbc = k.sb("bvbc", [128, 8 * 128], F32)
    kev = k.sb("kev", [128, TG], BF16)
    vev = k.sb("vev", [128, 128], BF16)

    k.gdma(lambda s: s.dma_start(
        out=wk.ap().rearrange("p (mb kc m) -> p mb kc m", mb=8, kc=8),
        in_=wkT.rearrange("mb kc p m -> p mb kc m")))
    k.gdma(lambda s: s.dma_start(
        out=wv.ap().rearrange("p (mb kc m) -> p mb kc m", mb=8, kc=8),
        in_=wvT.rearrange("mb kc p m -> p mb kc m")))
    k.sdma(lambda s: s.dma_start(out=bkc[:], in_=bkcol[:]))
    k.sdma(lambda s: s.dma_start(out=bvr[:], in_=bvrow[:]))
    for i in range(8):
        ps = k.psum()
        k.pe(lambda t, i=i, ps=ps: t.matmul(
            out=ps[0:128, 0:128], lhsT=C.row[:, 0:128],
            rhs=bvr[:, i * 128:(i + 1) * 128], start=True, stop=True))
        k.cp(bvbc[:, i * 128:(i + 1) * 128], ps[0:128, 0:128])

    ln = LN(k, "topln1", C)
    acols = lambda kc: acc[:, kc * TG:(kc + 1) * TG]
    xbcols = lambda kc: x1b[:, kc * TG:(kc + 1) * TG]

    for t0 in range(0, T, TG):
        for kc in range(8):
            k.sdma(lambda s, kc=kc, t0=t0: s.dma_start(
                out=acc[:, kc * TG:(kc + 1) * TG],
                in_=x_in[kc * 128:(kc + 1) * 128, t0:t0 + TG]))
            k.sdma(lambda s, kc=kc, t0=t0: s.dma_start(
                out=tmp[:], in_=xA_in[kc * 128:(kc + 1) * 128, t0:t0 + TG]))
            k.tt(acols(kc), acols(kc), tmp[:], ALU.add)
            k.sdma(lambda s, kc=kc, t0=t0: s.dma_start(
                out=tmp[:], in_=xF_in[kc * 128:(kc + 1) * 128, t0:t0 + TG]))
            k.tt(acols(kc), acols(kc), tmp[:], ALU.add)
            k.store(lambda s, kc=kc, t0=t0: s.dma_start(
                out=x1_out[kc * 128:(kc + 1) * 128, t0:t0 + TG], in_=acols(kc)))
        ln.stats_tg(acols, TG, sq)
        ln.plain_apply_tg(TG, acols, acols)
        for kc in range(8):
            k.store(lambda s, kc=kc, t0=t0: s.dma_start(
                out=xn1_out[kc * 128:(kc + 1) * 128, t0:t0 + TG], in_=acols(kc)))
            k.cp(xbcols(kc), acols(kc))
        for mb in range(8):
            ps = k.psum()
            def fk(t, mb=mb, ps=ps):
                last = None
                for kc in range(8):
                    last = t.matmul(out=ps[0:128, 0:TG],
                                    lhsT=wk[:, (mb * 8 + kc) * 128:(mb * 8 + kc + 1) * 128],
                                    rhs=x1b[:, kc * TG:(kc + 1) * TG],
                                    start=(kc == 0), stop=(kc == 7))
                return last
            k.pe(fk)
            k.ts(kev[:], ps[0:128, 0:TG], bkc[:, mb:mb + 1], ALU.add)
            k.store(lambda s, mb=mb, t0=t0: s.dma_start(
                out=K_out[mb * 128:(mb + 1) * 128, t0:t0 + TG], in_=kev[:]))
        for tt in range(4):
            for mb in range(8):
                ps = k.psum()
                def fv(t, tt=tt, mb=mb, ps=ps):
                    last = None
                    for kc in range(8):
                        last = t.matmul(
                            out=ps[0:128, 0:128],
                            lhsT=x1b[:, kc * TG + tt * 128:kc * TG + (tt + 1) * 128],
                            rhs=wv[:, (mb * 8 + kc) * 128:(mb * 8 + kc + 1) * 128],
                            start=(kc == 0), stop=(kc == 7))
                    return last
                k.pe(fv)
                k.tt(vev[:], ps[0:128, 0:128], bvbc[:, mb * 128:(mb + 1) * 128],
                     ALU.add)
                k.store(lambda s, tt=tt, mb=mb, t0=t0: s.dma_start(
                    out=V_out[t0 + tt * 128:t0 + (tt + 1) * 128,
                              mb * 128:(mb + 1) * 128], in_=vev[:]))
    k.finish()
    return k



def build_l1attn():
    k = K("l1attn")
    T = TOWN
    NH = 16
    xn1_in = k.din("xn1", [D, T])
    K_in = k.din("Kf", [D, NTOK], BF16)
    V_in = k.din("Vt", [NTOK, D], BF16)
    wqT = k.din("wqT", [8, 8, 128, 128])
    bqcol = k.din("bqcol", [128, 8])
    woT = k.din("woT", [8, 8, 128, 128])
    bocol = k.din("bocol", [128, 8])
    o1_out = k.dout("o1", [D, T])
    ctx_d = k.dint("ctx", [D, T], BF16)
    k.start()
    C = emit_consts(k)

    x1b = k.sb("x1b", [128, 8 * T], BF16)
    wqs = k.sb("wqs", [128, 8 * 8 * 128], BF16)
    wos = k.sb("wos", [128, 8 * 8 * 128], BF16)
    bqc = k.sb("bqc", [128, 8], F32)
    boc = k.sb("boc", [128, 8], F32)
    kh = k.sb("kh", [64, NTOK], BF16)
    qh = k.sb("qh", [64, T], BF16)
    vaug = k.sb("vaug", [128, 32 * 65], BF16)
    evs = k.sb("evs", [64, TG], F32)
    sqg = k.sb("sqg", [64, TG], F32)
    rowz = k.sb("rowz", [1, 2 * TG], F32)
    mrow = k.sb("mrow", [1, 4], F32)
    negm = k.sb("negm", [128, 1], F32)
    pt = k.sb("pt", [128, TG], BF16)
    rrow = k.sb("rrow", [1, TG], F32)
    rbt = k.sb("rbt", [64, TG], F32)
    ctxg = k.sb("ctxg", [128, 8 * TG], BF16)
    ctb = k.sb("ctb", [64, TG], BF16)
    oev = k.sb("oev", [128, TG], F32)

    k.gdma(lambda s: s.dma_start(
        out=wqs.ap().rearrange("p (mb kc m) -> p mb kc m", mb=8, kc=8),
        in_=wqT.rearrange("mb kc p m -> p mb kc m")))
    k.gdma(lambda s: s.dma_start(
        out=wos.ap().rearrange("p (mb kc m) -> p mb kc m", mb=8, kc=8),
        in_=woT.rearrange("mb kc p m -> p mb kc m")))
    k.sdma(lambda s: s.dma_start(out=bqc[:], in_=bqcol[:]))
    k.sdma(lambda s: s.dma_start(out=boc[:], in_=bocol[:]))
    for kc in range(8):
        k.gdma(lambda s, kc=kc: s.dma_start(
            out=x1b[:, kc * T:(kc + 1) * T],
            in_=xn1_in[kc * 128:(kc + 1) * 128, 0:T]))

    for h in range(NH):
        mbq, off = h // 2, 64 * (h % 2)
        k.sdma(lambda s, h=h: s.dma_start(out=kh[:], in_=K_in[h * 64:(h + 1) * 64, :]))
        k.gp(lambda s: s.memset(mrow[:, 0:2], 0.0))
        for tg in range(4):
            ps_q = k.psum()
            def fq(t, mbq=mbq, off=off, tg=tg, ps=ps_q):
                last = None
                for kc in range(8):
                    last = t.matmul(
                        out=ps[0:64, 0:TG],
                        lhsT=wqs[:, (mbq * 8 + kc) * 128 + off:(mbq * 8 + kc) * 128 + off + 64],
                        rhs=x1b[:, kc * T + tg * TG:kc * T + (tg + 1) * TG],
                        start=(kc == 0), stop=(kc == 7))
                return last
            k.pe(fq)
            k.ts(evs[:], ps_q[0:64, 0:TG], bqc[off:off + 64, mbq:mbq + 1], ALU.add)
            k.cp(qh[:, tg * TG:(tg + 1) * TG], evs[:])
            k.tt(sqg[:], evs[:], evs[:], ALU.mult)
            ps_n = k.psum()
            k.pe(lambda t, ps=ps_n: t.matmul(out=ps[0:1, 0:TG], lhsT=C.col[0:64, 0:1],
                                             rhs=sqg[:], start=True, stop=True))
            mr = rowz[:, 0:1]
            k.dve(lambda v: v.reduce_max(out=mr, in_=ps_n[0:1, 0:TG], axis=AX.X))
            k.tt(mrow[:, 0:1], mrow[:, 0:1], mr, ALU.max)
        for tg in range(8):
            k.cp(evs[:], kh[:, tg * TG:(tg + 1) * TG])
            k.tt(sqg[:], evs[:], evs[:], ALU.mult)
            ps_n = k.psum()
            k.pe(lambda t, ps=ps_n: t.matmul(out=ps[0:1, 0:TG], lhsT=C.col[0:64, 0:1],
                                             rhs=sqg[:], start=True, stop=True))
            mr = rowz[:, 0:1]
            k.dve(lambda v: v.reduce_max(out=mr, in_=ps_n[0:1, 0:TG], axis=AX.X))
            k.tt(mrow[:, 1:2], mrow[:, 1:2], mr, ALU.max)
        mm = mrow[:, 2:3]
        k.tt(mm, mrow[:, 0:1], mrow[:, 1:2], ALU.mult)
        k.actf(mm, mm, AF.Sqrt)
        k.ts(mm, mm, -0.125, ALU.mult)
        ps_nm = k.psum()
        k.pe(lambda t, ps=ps_nm: t.matmul(out=ps[0:128, 0:1], lhsT=C.row[:, 0:128],
                                          rhs=mm, start=True, stop=True))
        k.cp(negm[:], ps_nm[0:128, 0:1])
        k.gp(lambda s: s.memset(vaug[:], 1.0))
        for tt in range(32):
            k.sdma(lambda s, tt=tt, h=h: s.dma_start(
                out=vaug[:, tt * 65:tt * 65 + 64],
                in_=V_in[tt * 128:(tt + 1) * 128, h * 64:(h + 1) * 64]))
        for tg in range(4):
            ps_av = k.psum_pin()
            for kt in range(32):
                ps_s = k.psum()
                k.pe(lambda t, kt=kt, tg=tg, ps=ps_s: t.matmul(
                    out=ps[0:128, 0:TG], lhsT=kh[:, kt * 128:(kt + 1) * 128],
                    rhs=qh[:, tg * TG:(tg + 1) * TG], start=True, stop=True))
                k.actf(pt[:], ps_s[0:128, 0:TG], AF.Exp, bias=negm[:, 0:1],
                       scale=0.125)
                k.pe(lambda t, kt=kt, ps=ps_av: t.matmul(
                    out=ps[0:65, 0:TG], lhsT=vaug[:, kt * 65:(kt + 1) * 65],
                    rhs=pt[:], start=(kt == 0), stop=(kt == 31)))
            k.recip(rrow[:], ps_av[64:65, 0:TG])
            ps_rb = k.psum()
            k.pe(lambda t, ps=ps_rb: t.matmul(out=ps[0:64, 0:TG], lhsT=C.row[:, 0:64],
                                              rhs=rrow[:], start=True, stop=True))
            k.cp(rbt[:], ps_rb[0:64, 0:TG])
            k.cp(evs[:], ps_av[0:64, 0:TG])
            k.tt(evs[:], evs[:], rbt[:], ALU.mult)
            k.cp(ctb[:], evs[:])
            k.store(lambda s, h=h, tg=tg: s.dma_start(
                out=ctx_d[h * 64:(h + 1) * 64, tg * TG:(tg + 1) * TG],
                in_=ctb[:]), deps=("pe", "dve", "act", "gd"))
    for t0 in range(0, T, TG):
        for kc in range(8):
            k.sdma(lambda s, kc=kc, t0=t0: s.dma_start(
                out=ctxg[:, kc * TG:(kc + 1) * TG],
                in_=ctx_d[kc * 128:(kc + 1) * 128, t0:t0 + TG]),
                deps=("pe", "dve", "act", "st"))
        for mb in range(8):
            ps = k.psum()
            def fo(t, mb=mb, ps=ps):
                last = None
                for kc in range(8):
                    last = t.matmul(
                        out=ps[0:128, 0:TG],
                        lhsT=wos[:, (mb * 8 + kc) * 128:(mb * 8 + kc + 1) * 128],
                        rhs=ctxg[:, kc * TG:(kc + 1) * TG],
                        start=(kc == 0), stop=(kc == 7))
                return last
            k.pe(fo)
            k.ts(oev[:], ps[0:128, 0:TG], boc[:, mb:mb + 1], ALU.add)
            k.store(lambda s, mb=mb, t0=t0: s.dma_start(
                out=o1_out[mb * 128:(mb + 1) * 128, t0:t0 + TG], in_=oev[:]))
    k.finish()
    return k



def build_comb1():
    k = K("comb1")
    T = TOWN
    x1_in = k.din("x1", [D, T])
    o1_in = k.din("o1", [D, T])
    xF_in = k.din("xF", [D, T])
    x2_out = k.dout("x2", [D, T])
    k.start()
    acc = k.sb("acc", [128, TG], F32)
    tmp = k.sb("tmp", [128, TG], F32)
    for t0 in range(0, T, TG):
        for kc in range(8):
            k.sdma(lambda s, kc=kc, t0=t0: s.dma_start(
                out=acc[:], in_=x1_in[kc * 128:(kc + 1) * 128, t0:t0 + TG]))
            k.sdma(lambda s, kc=kc, t0=t0: s.dma_start(
                out=tmp[:], in_=o1_in[kc * 128:(kc + 1) * 128, t0:t0 + TG]))
            k.tt(acc[:], acc[:], tmp[:], ALU.add)
            k.sdma(lambda s, kc=kc, t0=t0: s.dma_start(
                out=tmp[:], in_=xF_in[kc * 128:(kc + 1) * 128, t0:t0 + TG]))
            k.tt(acc[:], acc[:], tmp[:], ALU.add)
            k.store(lambda s, kc=kc, t0=t0: s.dma_start(
                out=x2_out[kc * 128:(kc + 1) * 128, t0:t0 + TG], in_=acc[:]))
    k.finish()
    return k



_SGN4 = np.ascontiguousarray(np.tile(np.array([-1, 1, -1, 1], np.float32), (128, 1)))
_SGN2 = np.ascontiguousarray(np.tile(np.array([1, -1], np.float32), (128, 1)))
_REV64 = np.zeros((128, 64), np.float32)
for _l in range(64):
    _REV64[64 + _l, 63 - _l] = 1.0
_ROT = np.zeros((64, 64), np.float32)
for _m in range(32):
    _ROT[_m + 32, _m] = -1.0
for _m in range(32, 64):
    _ROT[_m - 32, _m] = 1.0
_LTRI = np.zeros((128, 128), np.float32)
for _s in range(128):
    _LTRI[_s, _s:] = 1.0


def tile_lhsT64(w):
    w = np.asarray(w, np.float32)
    Din, Dout = w.shape
    nHc = Din // 64
    nMb = Dout // 128
    out = np.zeros((nMb, nHc, 64, 128), np.float32)
    for mb in range(nMb):
        for hc in range(nHc):
            out[mb, hc] = w[hc * 64:(hc + 1) * 64, mb * 128:(mb + 1) * 128]
    return np.ascontiguousarray(out)


_progs = {}

def _prog(name):
    if name not in _progs:
        _progs[name] = globals()["build_" + name]()
    return _progs[name]


def _run(kk, in_maps):
    res = run_bass_kernel_spmd(kk.nc, in_maps, core_ids=list(range(NCORES)))
    return res.results


def kernel(**inputs):
    inp = {k: np.asarray(v) for k, v in inputs.items()}
    x_full = inp['L0_attn_x']
    x_fm = [fmh(x_full[b]) for b in range(4)]
    HID = 2730

    prep_w = {
        "ln1gb": gb_rows(inp['L0_attn_ln1_g'], inp['L0_attn_ln1_b']),
        "wpot": tile_lhsT(inp['L0_attn_w_pot']),
        "bpot": np.ascontiguousarray(inp['L0_attn_b_pot'][None, :]),
    }
    mid_w = {
        "wbk": tile_lhsT(inp['L0_attn_w_bk']),
        "bbk": col_bias(inp['L0_attn_b_bk'], 8),
        "ln2gb": gb_rows(inp['L0_attn_ln2_g'], inp['L0_attn_ln2_b']),
        "wqkv": tile_lhsT(inp['L0_attn_w_qkv']),
        "bqkv": col_bias(inp['L0_attn_b_qkv'], 12),
        "bqkv_row": np.ascontiguousarray(inp['L0_attn_b_qkv'][None, 512:1536]),
        "ltri": _LTRI,
    }
    a0_w = {
        "wout": tile_lhsT(inp['L0_attn_w_out']),
        "bout": col_bias(inp['L0_attn_b_out'], 8),
        "ln3gb": gb_rows(inp['L0_attn_ln3_g'], inp['L0_attn_ln3_b']),
        "wff1": tile_lhsT(inp['L0_attn_w_ff1']),
        "bff1": col_bias(inp['L0_attn_b_ff1'], 32),
        "wff2": tile_lhsT(inp['L0_attn_w_ff2']),
        "bff2": col_bias(inp['L0_attn_b_ff2'], 8),
    }

    def ffn_attn_w(pfx, layer):
        return {
            "wqkv": tile_lhsT(inp[pfx + 'wqkv'][layer]),
            "wo8": tile_lhsT64(inp[pfx + 'wo'][layer]),
            "qscol": np.ascontiguousarray(inp[pfx + 'q_scale'][layer][:, None]),
            "kscol": np.ascontiguousarray(inp[pfx + 'k_scale'][layer][:, None]),
            "rot64": _ROT,
        }

    def ffn_ffn_w(pfx, layer):
        w1 = inp[pfx + 'w1'][layer]
        b1 = inp[pfx + 'b1'][layer]
        return {
            "lngb": gb_rows(inp[pfx + 'ln_g'][layer], inp[pfx + 'ln_b'][layer]),
            "w1a": tile_lhsT(w1[:, 0:HID]),
            "w1g": tile_lhsT(w1[:, HID:2 * HID]),
            "b1a": col_bias(b1[0:HID], 22),
            "b1g": col_bias(b1[HID:2 * HID], 22),
            "w2": tile_lhsT(inp[pfx + 'w2'][layer]),
            "b2": col_bias(inp[pfx + 'b2'][layer], 8),
        }

    c0kv_w = {
        "wkT": tile_lhsT(np.ascontiguousarray(inp['L1_attn_Wk'].T)),
        "bkcol": col_bias(inp['L1_attn_bk'], 8),
        "wvT": tile_lhsT(np.ascontiguousarray(inp['L1_attn_Wv'].T)),
        "bvrow": np.ascontiguousarray(inp['L1_attn_bv'][None, :]),
    }
    l1a_w = {
        "wqT": tile_lhsT(np.ascontiguousarray(inp['L1_attn_Wq'].T)),
        "bqcol": col_bias(inp['L1_attn_bq'], 8),
        "woT": tile_lhsT(np.ascontiguousarray(inp['L1_attn_Wo'].T)),
        "bocol": col_bias(inp['L1_attn_bo'], 8),
    }

    def core_bh(c):
        return c // 2, c % 2

    def own(fm_arr_by_b, c):
        b, h = core_bh(c)
        return np.ascontiguousarray(fm_arr_by_b[b][:, 2048 * h:2048 * h + 2048])

    r_prep = _run(_prog("prep"), [dict(prep_w, x=x_fm[c // 2]) for c in range(8)])
    r_cf = _run(_prog("cf"), [
        {"d2": np.ascontiguousarray(r_prep[c]['d2']),
         "d2r": np.ascontiguousarray(r_prep[c]['d2r']),
         "sgn4": _SGN4, "sgn2": _SGN2, "rev64": _REV64} for c in range(8)])
    r_mid = _run(_prog("mid"), [
        dict(mid_w, xn=np.ascontiguousarray(r_prep[c]['xn']),
             Gp=np.ascontiguousarray(r_cf[c]['Gp']),
             gamma=np.ascontiguousarray(r_prep[c]['gamma'])) for c in range(8)])
    r_a0 = _run(_prog("a0end"), [
        dict(a0_w,
             a1=np.ascontiguousarray(r_mid[c]['a1'][:, 2048 * (c % 2):2048 * (c % 2) + 2048]),
             heb=np.ascontiguousarray(r_mid[c]['heb'][:, 2048 * (c % 2):2048 * (c % 2) + 2048]))
        for c in range(8)])

    def run_ffn_module(xin_by_core, pfx):
        cur = xin_by_core
        for layer in range(2):
            wa = ffn_attn_w(pfx, layer)
            wf = ffn_ffn_w(pfx, layer)
            im = []
            for c in range(8):
                b, h = core_bh(c)
                if h == 0:
                    xh = np.zeros((D, 513), np.float32)
                else:
                    xh = np.ascontiguousarray(cur[c - 1][:, 2048 - 513:2048])
                im.append(dict(wa, xin=np.ascontiguousarray(cur[c]), xhal=xh,
                               t0=np.array([[2048.0 * h]], np.float32),
                               pflag=np.array([[float(h)]], np.float32)))
            r_a = _run(_prog("ffn_attn"), im)
            im2 = []
            for c in range(8):
                b, h = core_bh(c)
                if h == 0:
                    pv = np.zeros((D, 1), np.float32)
                else:
                    pv = np.ascontiguousarray(r_a[c - 1]['ymid'][:, 2047:2048])
                im2.append(dict(wf, ymid=np.ascontiguousarray(r_a[c]['ymid']),
                                prevtok=pv))
            r_f = _run(_prog("ffn_ffn"), im2)
            cur = [r_f[c]['y'] for c in range(8)]
        return cur

    xn_own = [np.ascontiguousarray(
        r_prep[c]['xn'][:, 2048 * (c % 2):2048 * (c % 2) + 2048]) for c in range(8)]
    xF0 = run_ffn_module(xn_own, 'L0_ffn_')

    r_c0 = _run(_prog("comb0kv"), [
        dict(c0kv_w, x=own(x_fm, c),
             xA=np.ascontiguousarray(r_a0[c]['xA']),
             xF=np.ascontiguousarray(xF0[c])) for c in range(8)])

    im_l1 = []
    for c in range(8):
        b, h = core_bh(c)
        p = 2 * b
        Kfull = np.ascontiguousarray(
            np.concatenate([r_c0[p]['Kf'], r_c0[p + 1]['Kf']], axis=1))
        Vfull = np.ascontiguousarray(
            np.concatenate([r_c0[p]['Vt'], r_c0[p + 1]['Vt']], axis=0))
        im_l1.append(dict(l1a_w, xn1=np.ascontiguousarray(r_c0[c]['xn1']),
                          Kf=Kfull, Vt=Vfull))
    r_l1 = _run(_prog("l1attn"), im_l1)

    xn1_own = [r_c0[c]['xn1'] for c in range(8)]
    xF1 = run_ffn_module(xn1_own, 'L1_ffn_')

    r_c1 = _run(_prog("comb1"), [
        {"x1": np.ascontiguousarray(r_c0[c]['x1']),
         "o1": np.ascontiguousarray(r_l1[c]['o1']),
         "xF": np.ascontiguousarray(xF1[c])} for c in range(8)])

    out = np.empty((4, NTOK, D), np.float32)
    for c in range(8):
        b, h = core_bh(c)
        out[b, 2048 * h:2048 * h + 2048, :] = np.asarray(r_c1[c]['x2']).T
    return out
